# version 10
# baseline (speedup 1.0000x reference)
import numpy as np
from contextlib import ExitStack

import bass_rust
import concourse.bass as bass
import concourse.mybir as mybir
from concourse.bass import IndirectOffsetOnAxis
from concourse import library_config
from concourse.bass_utils import run_bass_kernel_spmd

F32 = mybir.dt.float32
I32 = mybir.dt.int32
I16 = mybir.dt.int16
U32 = mybir.dt.uint32
AF = mybir.ActivationFunctionType
ALU = mybir.AluOpType
AX = mybir.AxisListType
RED = bass_rust.ReduceOp

NCORES = 8
T = 16384
D = 1024
DCH = 8
FF = 4096
FFCH = 32
CSL = 2048
NT = 16
NLO = 8
NKB = 32
SB = 256
NSB = CSL // SB
INV_SQ_D = 1.0 / 32.0
EPS = 1e-5

VISITS = [(kb, j) for kb in range(NKB) for j in range(kb // 2, NT)]
NVIS = len(VISITS)
LIMCOLS = 272


def build_nc(nphases=6, dbg=(), cut=None, ablate=()):
    nc = bass.Bass("TRN2", target_bir_lowering=False, detect_race_conditions=False)

    x_in = nc.dram_tensor("x_in", [2048, D], F32, kind="ExternalInput").ap()
    wq_in = nc.dram_tensor("wq", [D, D], F32, kind="ExternalInput").ap()
    wk_in = nc.dram_tensor("wk", [D, D], F32, kind="ExternalInput").ap()
    wv_in = nc.dram_tensor("wv", [D, D], F32, kind="ExternalInput").ap()
    bq_in = nc.dram_tensor("bq", [D], F32, kind="ExternalInput").ap()
    bk_in = nc.dram_tensor("bk", [D], F32, kind="ExternalInput").ap()
    bv_in = nc.dram_tensor("bv", [D], F32, kind="ExternalInput").ap()
    w1_in = nc.dram_tensor("w1", [D, FF], F32, kind="ExternalInput").ap()
    b1_in = nc.dram_tensor("b1", [FF], F32, kind="ExternalInput").ap()
    w2_in = nc.dram_tensor("w2", [FF, D], F32, kind="ExternalInput").ap()
    b2_in = nc.dram_tensor("b2", [D], F32, kind="ExternalInput").ap()
    cw_in = nc.dram_tensor("cw", [8192, 64], F32, kind="ExternalInput").ap()
    oh_in = nc.dram_tensor("oh128", [1024], F32, kind="ExternalInput").ap()
    idx_in = nc.dram_tensor("idxc", [CSL], I32, kind="ExternalInput").ap()
    lim_in = nc.dram_tensor("limits", [LIMCOLS], F32, kind="ExternalInput").ap()
    pt_in = nc.dram_tensor("ptab", [8192, 64], I32, kind="ExternalInput").ap()

    x_out = nc.dram_tensor("x_out", [2048, D], F32, kind="ExternalOutput").ap()

    y_agin = nc.dram_tensor("y_agin", [1024, D], F32).ap()
    y_all = nc.dram_tensor("y_all", [8192, D], F32).ap()
    kv_agin_lo = nc.dram_tensor("kv_agin_lo", [2048, 1024], F32).ap()
    kv_agin_hi = nc.dram_tensor("kv_agin_hi", [2048, 1024], F32).ap()
    kv_all_lo = nc.dram_tensor("kv_all_lo", [T, 1024], F32).ap()
    kv_all_hi = nc.dram_tensor("kv_all_hi", [T, 1024], F32).ap()
    pout = nc.dram_tensor("pout", [8192, D], F32).ap()
    rs_out = nc.dram_tensor("rs_out", [1024, D], F32).ap()
    xspill = nc.dram_tensor("xspill", [2048, D], F32).ap()
    mz_in = nc.dram_tensor("mz_in", [1, 2], F32).ap()
    mz_all = nc.dram_tensor("mz_all", [8, 2], F32).ap()
    bscr = nc.dram_tensor("bscr", [1, 2], F32).ap()
    pspill = nc.dram_tensor("pspill", [2, CSL], F32).ap()

    dbg_aps = {}
    for name, shape in (
        ("dbg_yall", [8192, D]), ("dbg_rs", [1024, D]), ("dbg_pout", [8192, D]),
        ("dbg_kall", [8 * D, 2048]), ("dbg_vall", [T, D]), ("dbg_q", [128, 16384]),
        ("dbg_ml", [128, 32]), ("dbg_cnt", [128, 48]),
        ("dbg_h", [128, 16384]), ("dbg_gt", [128, 16384]),
    ):
        if name in dbg:
            dbg_aps[name] = nc.dram_tensor(name, shape, F32, kind="ExternalOutput").ap()

    st = ExitStack()
    with st:
        e = st.enter_context

        xs = e(nc.sbuf_tensor("xs", [128, NT * D], F32))
        big = e(nc.sbuf_tensor("big", [128, 16384], F32))
        qf = e(nc.sbuf_tensor("qf", [128, 16384], F32))

        pm_sb = e(nc.sbuf_tensor("pm_sb", [128, 1536], F32))
        pt_sb = e(nc.sbuf_tensor("pt_sb", [128, 512], F32))
        iota_f = e(nc.sbuf_tensor("iota_f", [128, 512], F32))
        lim_sb = e(nc.sbuf_tensor("lim_sb", [128, LIMCOLS], F32))
        M_sb = e(nc.sbuf_tensor("M_sb", [128, NT], F32))
        nM_sb = e(nc.sbuf_tensor("nM_sb", [128, NT], F32))
        L_sb = e(nc.sbuf_tensor("L_sb", [128, NT], F32))
        cnt_sb = e(nc.sbuf_tensor("cnt_sb", [128, NT], F32))
        w_sb = e(nc.sbuf_tensor("w_sb", [128, NT], F32))
        scl_sb = e(nc.sbuf_tensor("scl_sb", [128, NT], F32))
        sm1 = e(nc.sbuf_tensor("sm1", [128, 1], F32))
        sm2 = e(nc.sbuf_tensor("sm2", [128, 1], F32))
        sm3 = e(nc.sbuf_tensor("sm3", [128, 1], F32))
        sm4 = e(nc.sbuf_tensor("sm4", [128, 1], F32))
        sm5 = e(nc.sbuf_tensor("sm5", [128, 1], F32))
        sm6 = e(nc.sbuf_tensor("sm6", [128, 1], F32))
        sm7 = e(nc.sbuf_tensor("sm7", [128, 1], F32))
        sm8 = e(nc.sbuf_tensor("sm8", [128, 1], F32))
        b1s = e(nc.sbuf_tensor("b1s", [128, FFCH], F32))
        ident = e(nc.sbuf_tensor("ident", [128, 128], F32))
        ones1 = e(nc.sbuf_tensor("ones1", [128, 1], F32))
        bc2 = e(nc.sbuf_tensor("bc2", [128, 2], F32))

        pb = [e(nc.psum_tensor(f"pb{i}", [128, 512], F32)) for i in range(8)]

        class Sem:
            def __init__(self, name):
                self.h = e(nc.semaphore(name))
                self.n = 0

        sS = Sem("sS")
        sG = Sem("sG")
        sCC = Sem("sCC")
        sT = Sem("sT")
        sV = Sem("sV")
        sC = Sem("sC")
        sP = Sem("sP")

        bqkv = pm_sb[:, 1056:1080]
        idx16p = pt_sb[:, 0:64].bitcast(I16)
        ixu = pt_sb[:, 128:144].bitcast(U32)
        idx16r = pt_sb[:, 64:128].bitcast(I16)
        f_sb = pm_sb[:, 48:64]
        ssq = sm5
        mzrow = pm_sb[0:1, 0:16]
        mzred = pm_sb[0:1, 16:18]
        mz8 = pm_sb[0:1, 24:32]
        mzs = pm_sb[0:1, 32:33]
        block = e(nc.Block())

        def xs3(j):
            return xs[:, j * D:(j + 1) * D]

        BIG_H = 0
        BIG_W1 = 8192
        BIG_Z = 11264
        BIG_SQ = 12288
        BIG_ST = 13312
        BIG_B2 = 14336
        BIG_ONE = 15360

        QF_GT = 0
        QF_G = 4096
        QF_SC = 8192
        QF_CMP = 10240
        QF_IB = 12288
        QF_WA = 14336

        @block.sync
        def _(s):
            s.dma_start(out=xs[:].rearrange("p (j d) -> p j d", d=D),
                        in_=x_in.rearrange("(j p) d -> p j d", p=128)).then_inc(sS.h, 16)
            s.dma_start(out=bqkv[:, 0:8], in_=bq_in.rearrange("(o p) -> p o", p=128), allow_slow_non_contiguous=True).then_inc(sS.h, 16)
            s.dma_start(out=bqkv[:, 8:16], in_=bk_in.rearrange("(o p) -> p o", p=128), allow_slow_non_contiguous=True).then_inc(sS.h, 16)
            s.dma_start(out=bqkv[:, 16:24], in_=bv_in.rearrange("(o p) -> p o", p=128), allow_slow_non_contiguous=True).then_inc(sS.h, 16)
            s.dma_start(out=b1s[:], in_=b1_in.rearrange("(o p) -> p o", p=128), allow_slow_non_contiguous=True).then_inc(sS.h, 16)
            s.dma_start(out=lim_sb[:], in_=lim_in.rearrange("(p d) -> p d", p=1)
                        .to_broadcast((128, LIMCOLS))).then_inc(sS.h, 16)
            s.dma_start(out=qf[:, 0:1024], in_=oh_in.rearrange("(p d) -> p d", p=1)
                        .to_broadcast((128, 1024))).then_inc(sS.h, 16)
        sS.n += 7 * 16
        P0_SYNC = sS.n

        @block.gpsimd
        def _(g):
            g.iota(pm_sb[:, 0:512].bitcast(I32), pattern=[[1, 512]], base=0, channel_multiplier=-1)
            g.nop().then_inc(sP.h, 1)
            g.load_library(library_config.mlp)
            g.dma_start(out=idx16r[0:16, :], in_=idx_in.rearrange("(s p) -> p s", p=16),
                        allow_slow_non_contiguous=True).then_inc(sG.h, 16)
            g.wait_ge(sG.h, sG.n + 16)
            for gi in range(1, 8):
                g.dma_start(out=idx16r[16 * gi:16 * (gi + 1), :],
                            in_=idx16r[0:16, :]).then_inc(sG.h, 16)
            g.wait_ge(sG.h, sG.n + 128)
            g.dma_gather(out_ap=qf[:, QF_WA:QF_WA + 1024].rearrange("p (b w) -> p b w", w=64),
                         in_ap=cw_in, idxs_ap=idx16r[:], num_idxs=CSL,
                         num_idxs_reg=CSL, elem_size=64,
                         single_packet=False).then_inc(sG.h, 16)
            g.dma_gather(out_ap=qf[:, QF_IB:QF_IB + 1024].bitcast(I32)
                         .rearrange("p (b w) -> p b w", w=64),
                         in_ap=pt_in, idxs_ap=idx16r[:], num_idxs=CSL,
                         num_idxs_reg=CSL, elem_size=64,
                         single_packet=False).then_inc(sG.h, 16)
            g.nop().then_inc(sP.h, 1)
        sG.n += 10 * 16
        sP.n += 2
        P0_GP = sP.n
        P0_G2 = sG.n

        @block.vector
        def _(v):
            v.wait_ge(sP.h, P0_GP)
            v.wait_ge(sS.h, P0_SYNC)
            v.wait_ge(sG.h, P0_G2)
            v.tensor_copy(iota_f[:], pm_sb[:, 0:512].bitcast(I32))
            v.drain()
            v.tensor_scalar(ident[:], iota_f[:, 0:128], 0.0, None, ALU.is_equal)
            v.tensor_copy(w_sb[:], qf[:, QF_WA:QF_WA + 1024]
                          .rearrange("p (b w) -> p b w", w=64)[:, :, 9])
            v.drain()
            v.nop().then_inc(sV.h, 1)
        sV.n += 1
        P0_VEC = sV.n

        @block.gpsimd
        def _(g):
            g.wait_ge(sV.h, P0_VEC)
            g.dma_start(out=pspill[0:1, :].bitcast(I32).rearrange("q (b p) -> (q p) b", p=128),
                        in_=qf[:, QF_IB:QF_IB + 1024].bitcast(I32)
                        .rearrange("p (b w) -> p b w", w=64)[:, :, 0],
                        allow_slow_non_contiguous=True).then_inc(sG.h, 16)
            g.dma_start(out=pspill[1:2, :].rearrange("q (b p) -> (q p) b", p=128),
                        in_=w_sb[:], allow_slow_non_contiguous=True).then_inc(sG.h, 16)
            g.wait_ge(sG.h, sG.n + 32)
            g.dma_start(out=ixu[:],
                        in_=pspill[0:1, :].bitcast(I32).rearrange("q (b p) -> (q p) b", p=128),
                        allow_slow_non_contiguous=True).then_inc(sG.h, 16)
            g.dma_start(out=idx16p[0:16, :],
                        in_=pspill[0:1, :].bitcast(I32).rearrange("q (s p) -> (q p) s", p=16),
                        allow_slow_non_contiguous=True).then_inc(sG.h, 16)
            g.dma_start(out=qf[:, QF_CMP:QF_CMP + CSL],
                        in_=pspill[1:2, :].to_broadcast((128, CSL))).then_inc(sG.h, 16)
            g.wait_ge(sG.h, sG.n + 80)
            for gi in range(1, 8):
                g.dma_start(out=idx16p[16 * gi:16 * (gi + 1), :],
                            in_=idx16p[0:16, :]).then_inc(sG.h, 16)
            g.nop().then_inc(sP.h, 1)
        sG.n += 12 * 16
        sP.n += 1
        P0_G3 = sG.n
        P0_GP2 = sP.n

        @block.vector
        def _(v):
            v.wait_ge(sG.h, P0_G3)
            v.wait_ge(sP.h, P0_GP2)
            for m in range(NT):
                v.tensor_scalar(qf[:, QF_IB:QF_IB + CSL], qf[:, QF_CMP:QF_CMP + CSL],
                                w_sb[:, m:m + 1], None, ALU.subtract)
                v.drain()
                v.tensor_scalar(qf[:, QF_IB:QF_IB + CSL], qf[:, QF_IB:QF_IB + CSL],
                                0.0, None, ALU.is_equal)
                v.drain()
                v.tensor_reduce(cnt_sb[:, m:m + 1], qf[:, QF_IB:QF_IB + CSL],
                                AX.X, ALU.add)
            v.tensor_tensor(qf[:, QF_WA:QF_WA + 1024], qf[:, QF_WA:QF_WA + 1024],
                            qf[:, 0:1024], ALU.mult)
            v.drain()
            for m in range(NT):
                v.tensor_reduce(w_sb[:, m:m + 1], qf[:, QF_WA + 64 * m: QF_WA + 64 * (m + 1)],
                                AX.X, ALU.add)
            v.drain()
            v.tensor_tensor(scl_sb[:], w_sb[:], cnt_sb[:], ALU.mult)
            v.drain()
            v.nop().then_inc(sV.h, 1)
        sV.n += 1
        P0_VEC2 = sV.n

        if "dbg_cnt" in dbg_aps:
            @block.sync
            def _(s):
                s.wait_ge(sV.h, P0_VEC2)
                s.dma_start(out=dbg_aps["dbg_cnt"][:, 0:16], in_=cnt_sb[:]).then_inc(sS.h, 16)
                s.dma_start(out=dbg_aps["dbg_cnt"][:, 16:32], in_=w_sb[:]).then_inc(sS.h, 16)
                s.dma_start(out=dbg_aps["dbg_cnt"][:, 32:48],
                            in_=ixu[:].bitcast(F32)).then_inc(sS.h, 16)
                sS.n += 48

        def emit_ln(j, dst_off):
            @block.scalar
            def _(s):
                s.activation(big[:, BIG_SQ:BIG_SQ + D], xs3(j), AF.Square)
                s.drain()
                s.nop().then_inc(sC.h, 1)
            sC.n += 1
            sq_done = sC.n

            @block.vector
            def _(v):
                v.wait_ge(sC.h, sq_done)
                v.tensor_reduce(ssq[:], big[:, BIG_SQ:BIG_SQ + D], AX.X, ALU.add)
                v.tensor_reduce(sm6[:], xs3(j), AX.X, ALU.add)
                v.drain()
                v.tensor_scalar(sm6[:], sm6[:], 1.0 / D, None, ALU.mult)
                v.tensor_scalar(sm8[:], ssq[:], 1.0 / D, None, ALU.mult)
                v.drain()
                v.tensor_tensor(sm7[:], sm6[:], sm6[:], ALU.mult)
                v.drain()
                v.tensor_tensor(sm8[:], sm8[:], sm7[:], ALU.subtract)
                v.drain()
                v.tensor_scalar(sm8[:], sm8[:], EPS, None, ALU.add)
                v.drain()
                v.nop().then_inc(sV.h, 1)
            sV.n += 1
            var_done = sV.n

            @block.scalar
            def _(s):
                s.wait_ge(sV.h, var_done)
                s.activation(sm7[:], sm8[:], AF.Sqrt)
                s.drain()
                s.nop().then_inc(sC.h, 1)
            sC.n += 1
            std_done = sC.n

            @block.vector
            def _(v):
                v.wait_ge(sC.h, std_done)
                v.reciprocal(sm8[:], sm7[:])
                v.drain()
                v.scalar_tensor_tensor(sm7[:], sm6[:], -1.0, sm8[:], ALU.mult, ALU.mult)
                v.drain()
                v.nop().then_inc(sV.h, 1)
            sV.n += 1
            r_done = sV.n

            @block.scalar
            def _(s):
                s.wait_ge(sV.h, r_done)
                s.activation(big[:, dst_off:dst_off + D], xs3(j), AF.Identity,
                             bias=sm7[:], scale=sm8[:])
                s.drain()
                s.nop().then_inc(sC.h, 1)
            sC.n += 1
            return sC.n

        def emit_ln_stats(n, pre_sync=None):
            @block.vector
            def _(v):
                if pre_sync is not None:
                    v.wait_ge(sS.h, pre_sync)
                for j in range(n):
                    scr = BIG_SQ if j % 2 == 0 else BIG_ST
                    v.tensor_tensor(big[:, scr:scr + D], xs3(j), xs3(j), ALU.mult)
                    v.drain()
                    v.tensor_reduce(L_sb[:, j:j + 1], big[:, scr:scr + D], AX.X, ALU.add)
                    v.tensor_reduce(M_sb[:, j:j + 1], xs3(j), AX.X, ALU.add)
                    v.drain()
                v.tensor_scalar(M_sb[:, 0:n], M_sb[:, 0:n], 1.0 / D, None, ALU.mult)
                v.tensor_scalar(L_sb[:, 0:n], L_sb[:, 0:n], 1.0 / D, None, ALU.mult)
                v.drain()
                v.tensor_tensor(cnt_sb[:, 0:n], M_sb[:, 0:n], M_sb[:, 0:n], ALU.mult)
                v.drain()
                v.tensor_tensor(L_sb[:, 0:n], L_sb[:, 0:n], cnt_sb[:, 0:n], ALU.subtract)
                v.drain()
                v.tensor_scalar(L_sb[:, 0:n], L_sb[:, 0:n], EPS, None, ALU.add)
                v.drain()
                v.nop().then_inc(sV.h, 1)
            sV.n += 1
            var_done = sV.n

            @block.scalar
            def _(s):
                s.wait_ge(sV.h, var_done)
                s.activation(cnt_sb[:, 0:n], L_sb[:, 0:n], AF.Sqrt)
                s.drain()
                s.nop().then_inc(sC.h, 1)
            sC.n += 1
            std_done = sC.n

            @block.vector
            def _(v):
                v.wait_ge(sC.h, std_done)
                v.reciprocal(nM_sb[:, 0:n], cnt_sb[:, 0:n])
                v.drain()
                v.scalar_tensor_tensor(cnt_sb[:, 0:n], M_sb[:, 0:n], -1.0,
                                       nM_sb[:, 0:n], ALU.mult, ALU.mult)
                v.drain()
                v.nop().then_inc(sV.h, 1)
            sV.n += 1
            return sV.n

        def emit_ffn(tag, cut=None):
            @block.gpsimd
            def _(g):
                g.wait_ge(sV.h, sV.n)
                g.wait_ge(sC.h, sC.n)
                g.memset(qf[:, 0:8192], 0.0)
                g.nop().then_inc(sP.h, 1)
                g.wait_ge(sP.h, sP.n + 1)
                for a in range(8):
                    g.dma_start(
                        out=pout[a * 1024:(a + 1) * 1024, :].rearrange("(b p) d -> p b d", p=128),
                        in_=qf[:, 0:8192].rearrange("p (b d) -> p b d", d=D),
                    ).then_inc(sG.h, 16)
            sP.n += 1
            sG.n += 8 * 16
            zero_done = sG.n

            @block.sync
            def _(s):
                s.dma_start(out=big[:, BIG_B2:BIG_B2 + D],
                            in_=b2_in.rearrange("(p d) -> p d", p=1)
                            .to_broadcast((128, D))).then_inc(sS.h, 16)
            sS.n += 16
            b2_done = sS.n

            if cut == "zero":
                return True
            if tag == "f2":
                @block.gpsimd
                def _(g):
                    g.wait_ge(sV.h, sV.n)
                    g.dma_start(out=idx16p[0:16, :],
                                in_=pspill[0:1, :].bitcast(I32).rearrange("q (s p) -> (q p) s", p=16),
                                allow_slow_non_contiguous=True).then_inc(sG.h, 16)
                    g.dma_start(out=ixu[:],
                                in_=pspill[0:1, :].bitcast(I32).rearrange("q (b p) -> (q p) b", p=128),
                                allow_slow_non_contiguous=True).then_inc(sG.h, 16)
                    g.wait_ge(sG.h, sG.n + 32)
                    for gi in range(1, 8):
                        g.dma_start(out=idx16p[16 * gi:16 * (gi + 1), :],
                                    in_=idx16p[0:16, :]).then_inc(sG.h, 16)
                sG.n += 9 * 16
                zero_done = sG.n

            if tag == "f1":
                ln_stats_done = emit_ln_stats(NLO, pre_sync=P0_SYNC)
                ydma = {}
                for j in range(NLO):
                    zofs = BIG_Z if j % 2 == 0 else BIG_ST

                    @block.scalar
                    def _(s, j=j, zofs=zofs, ln_stats_done=ln_stats_done):
                        if j == 0:
                            s.wait_ge(sV.h, ln_stats_done)
                        if j >= 2:
                            s.wait_ge(sS.h, ydma[j - 2])
                        s.activation(big[:, zofs:zofs + D], xs3(j), AF.Identity,
                                     bias=cnt_sb[:, j:j + 1], scale=nM_sb[:, j:j + 1])
                        s.drain()
                        s.nop().then_inc(sC.h, 1)
                    sC.n += 1
                    y_done = sC.n

                    @block.sync
                    def _(s, j=j, y_done=y_done, zofs=zofs):
                        s.wait_ge(sC.h, y_done)
                        s.dma_start(out=y_agin[j * 128:(j + 1) * 128, :],
                                    in_=big[:, zofs:zofs + D]).then_inc(sS.h, 16)
                    sS.n += 16
                    ydma[j] = sS.n
            else:
                for j in range(NLO):
                    @block.scalar
                    def _(s, j=j):
                        if j == 0:
                            s.wait_ge(sV.h, sV.n)
                        else:
                            s.wait_ge(sS.h, sS.n)
                    y_done = emit_ln(j, BIG_Z)

                    @block.sync
                    def _(s, j=j, y_done=y_done):
                        s.wait_ge(sC.h, y_done)
                        s.dma_start(out=y_agin[j * 128:(j + 1) * 128, :],
                                    in_=big[:, BIG_Z:BIG_Z + D]).then_inc(sS.h, 16)
                    sS.n += 16

            if cut == "ln":
                return True
            @block.gpsimd
            def _(g):
                g.wait_ge(sS.h, sS.n)
                g.collective_compute("AllGather", ALU.bypass,
                                     replica_groups=[list(range(NCORES))],
                                     ins=[y_agin], outs=[y_all]).then_inc(sCC.h, 1)
            sCC.n += 1
            ag_done = sCC.n
            if cut == "ag":
                return True

            if "dbg_yall" in dbg_aps and tag == "f1":
                @block.gpsimd
                def _(g):
                    g.wait_ge(sCC.h, ag_done)
                    for a in range(8):
                        g.dma_start(
                            out=dbg_aps["dbg_yall"][a * 1024:(a + 1) * 1024, :]
                                .rearrange("(b p) d -> p b d", p=128),
                            in_=y_all[a * 1024:(a + 1) * 1024, :]
                                .rearrange("(b p) d -> p b d", p=128),
                        ).then_inc(sG.h, 16)
                sG.n += 8 * 16

            gelu_dones = []
            mm1_dones = []
            mm2_dones = []
            out_evac_done = sC.n
            tr_evac = {}
            blk_tr_last = {}
            blk_add_last = {-1: sV.n}
            blk_scat_last = {-1: sG.n}
            g_dones = {}

            def emit_gather(b):
                @block.gpsimd
                def _(g, b=b):
                    g.wait_ge(sCC.h, ag_done)
                    if b == 0:
                        g.wait_ge(sG.h, zero_done)
                        g.wait_ge(sV.h, P0_VEC2)
                    elif b >= 2:
                        g.wait_ge(sT.h, blk_tr_last[b - 2])
                    g.dma_gather(
                        out_ap=qf[:, QF_G + 2048 * (b % 2):
                                     QF_G + 2048 * (b % 2) + 2 * D]
                            .rearrange("p (b d) -> p b d", d=D),
                        in_ap=y_all, idxs_ap=idx16p[:, 16 * b:16 * (b + 1)],
                        num_idxs=SB, num_idxs_reg=SB, elem_size=D,
                        single_packet=False).then_inc(sG.h, 16)
                sG.n += 16
                g_dones[b] = sG.n

            emit_gather(0)
            for b in range(NSB):
                gofs = 2048 * (b % 2)

                for grp in range(4):
                    bank_i = 4 + (grp % 2)

                    @block.tensor
                    def _(t, b=b, grp=grp, bank_i=bank_i, gofs=gofs):
                        t.wait_ge(sG.h, g_dones[b])
                        if b == 0 and grp == 0:
                            t.wait_ge(sV.h, P0_VEC)
                            t.wait_ge(sV.h, blk_add_last[-1])
                        if bank_i in tr_evac:
                            t.wait_ge(sV.h, tr_evac[bank_i])
                        if grp < 2 and len(gelu_dones) >= 4 - grp:
                            t.wait_ge(sC.h, gelu_dones[-(4 - grp)])
                        for u in range(4):
                            a4 = grp * 4 + u
                            a_, s_ = a4 >> 1, a4 & 1
                            instr = t.transpose(
                                pb[bank_i][:, u * 128:(u + 1) * 128],
                                qf[:, QF_G + gofs + s_ * D + a_ * 128:
                                       QF_G + gofs + s_ * D + (a_ + 1) * 128],
                                ident[:])
                        instr.then_inc(sT.h, 1)
                    sT.n += 1
                    tr_done = sT.n

                    @block.vector
                    def _(v, grp=grp, bank_i=bank_i, tr_done=tr_done, gofs=gofs):
                        v.wait_ge(sT.h, tr_done)
                        a0 = (grp * 4) >> 1
                        v.tensor_copy(qf[:, QF_GT + gofs + a0 * SB:
                                             QF_GT + gofs + a0 * SB + 512],
                                      pb[bank_i][:])
                        v.drain()
                        v.nop().then_inc(sV.h, 1)
                    sV.n += 1
                    tr_evac[bank_i] = sV.n
                gt_done = sV.n
                blk_tr_last[b] = sT.n
                if b + 1 < NSB:
                    emit_gather(b + 1)

                for f in range(FFCH):
                    @block.sync
                    def _(s, f=f, b=b, nmm2=len(mm2_dones)):
                        if nmm2 >= 2:
                            s.wait_ge(sT.h, mm2_dones[nmm2 - 2])
                        s.dma_start(
                            out=big[:, BIG_W1 + (f % 2) * 1024: BIG_W1 + (f % 2) * 1024 + 1024]
                                .rearrange("p (a w) -> p a w", w=128),
                            in_=w1_in[:, f * 128:(f + 1) * 128].rearrange("(a p) w -> p a w", p=128),
                        ).then_inc(sS.h, 16)
                        s.dma_start(
                            out=big[:, 10240 + (f % 2) * 1024: 10240 + (f % 2) * 1024 + 1024],
                            in_=w2_in[f * 128:(f + 1) * 128, :],
                        ).then_inc(sS.h, 16)
                    sS.n += 32
                    w_done = sS.n

                    @block.tensor
                    def _(t, b=b, f=f, w_done=w_done, gt_done=gt_done, gofs=gofs,
                          ngelu=len(gelu_dones)):
                        t.wait_ge(sS.h, w_done)
                        if f == 0:
                            t.wait_ge(sV.h, gt_done)
                            if b == 0:
                                t.wait_ge(sC.h, out_evac_done)
                        if ngelu >= 4:
                            t.wait_ge(sC.h, gelu_dones[ngelu - 4])
                        hb = pb[4 + (f % 4)]
                        for a in range(DCH):
                            instr = t.matmul(
                                hb[:, 0:SB],
                                big[:, BIG_W1 + (f % 2) * 1024 + a * 128:
                                       BIG_W1 + (f % 2) * 1024 + (a + 1) * 128],
                                qf[:, QF_GT + gofs + a * SB: QF_GT + gofs + (a + 1) * SB],
                                start=(a == 0), stop=(a == DCH - 1))
                        instr.then_inc(sT.h, 1)
                    sT.n += 1
                    mm1_dones.append(sT.n)

                    @block.scalar
                    def _(s, f=f, mm1_done=sT.n):
                        s.wait_ge(sT.h, mm1_done)
                        s.activation(big[:, BIG_H + f * SB: BIG_H + (f + 1) * SB],
                                     pb[4 + (f % 4)][:, 0:SB], AF.Gelu_apprx_tanh,
                                     bias=b1s[:, f:f + 1])
                        s.drain()
                        s.nop().then_inc(sC.h, 1)
                    sC.n += 1
                    gelu_dones.append(sC.n)

                    @block.tensor
                    def _(t, b=b, f=f, gdone=gelu_dones[-1]):
                        t.wait_ge(sC.h, gdone)
                        if f == 0:
                            t.wait_ge(sV.h, blk_add_last[b - 1])
                        for s_ in range(2):
                            for h_ in range(2):
                                instr = t.matmul(
                                    pb[s_ * 2 + h_][:],
                                    big[:, BIG_H + f * SB + s_ * 128:
                                           BIG_H + f * SB + (s_ + 1) * 128],
                                    big[:, 10240 + (f % 2) * 1024 + h_ * 512:
                                            10240 + (f % 2) * 1024 + (h_ + 1) * 512],
                                    start=(f == 0), stop=(f == FFCH - 1))
                        instr.then_inc(sT.h, 1)
                    sT.n += 1
                    mm2_dones.append(sT.n)
                mm2_done = sT.n

                for s_ in range(2):
                    @block.vector
                    def _(v, b=b, s_=s_, mm2_done=mm2_done, scn=sC.n):
                        v.wait_ge(sT.h, mm2_done)
                        if b == 0 and s_ == 0:
                            v.wait_ge(sS.h, b2_done)
                        if s_ == 0:
                            v.wait_ge(sC.h, scn)
                        else:
                            v.wait_ge(sC.h, sC.n)
                        for h_ in range(2):
                            v.tensor_tensor(
                                qf[:, QF_CMP + h_ * 512: QF_CMP + (h_ + 1) * 512],
                                pb[s_ * 2 + h_][:],
                                big[:, BIG_B2 + h_ * 512: BIG_B2 + (h_ + 1) * 512],
                                ALU.add)
                        v.drain()
                        v.nop().then_inc(sV.h, 1)
                    sV.n += 1
                    add_done = sV.n

                    @block.scalar
                    def _(s, b=b, s_=s_, add_done=add_done):
                        s.wait_ge(sV.h, add_done)
                        s.wait_ge(sG.h, blk_scat_last[b - 1])
                        for h_ in range(2):
                            s.activation(
                                qf[:, QF_SC + s_ * D + h_ * 512: QF_SC + s_ * D + (h_ + 1) * 512],
                                qf[:, QF_CMP + h_ * 512: QF_CMP + (h_ + 1) * 512],
                                AF.Identity,
                                scale=scl_sb[:, 2 * b + s_: 2 * b + s_ + 1])
                        s.drain()
                        s.nop().then_inc(sC.h, 1)
                    sC.n += 1
                    ev_done = sC.n

                    @block.gpsimd
                    def _(g, b=b, s_=s_, ev_done=ev_done):
                        g.wait_ge(sC.h, ev_done)
                        if "noscat" in ablate:
                            g.dma_start(
                                out=pout[(2 * b + s_) * 128:(2 * b + s_ + 1) * 128, :],
                                in_=qf[:, QF_SC + s_ * D: QF_SC + (s_ + 1) * D]).then_inc(sG.h, 16)
                        else:
                            g.indirect_dma_start(
                                out=pout,
                                out_offset=IndirectOffsetOnAxis(
                                    ixu[:, 2 * b + s_: 2 * b + s_ + 1], 0),
                                in_=qf[:, QF_SC + s_ * D: QF_SC + (s_ + 1) * D],
                                in_offset=None).then_inc(sG.h, 16)
                    sG.n += 16
                blk_add_last[b] = sV.n
                blk_scat_last[b] = sG.n

            if "dbg_h" in dbg_aps and tag == "f1":
                @block.sync
                def _(s):
                    s.wait_ge(sC.h, sC.n)
                    s.wait_ge(sV.h, sV.n)
                    s.dma_start(out=dbg_aps["dbg_h"][:, 0:8192], in_=big[:, 0:8192]).then_inc(sS.h, 16)
                    s.dma_start(out=dbg_aps["dbg_gt"][:, 0:8192], in_=qf[:, 0:8192]).then_inc(sS.h, 16)
                sS.n += 32

            @block.gpsimd
            def _(g):
                g.wait_ge(sG.h, sG.n)
                g.collective_compute("ReduceScatter", ALU.add,
                                     replica_groups=[list(range(NCORES))],
                                     ins=[pout], outs=[rs_out]).then_inc(sCC.h, 1)
            sCC.n += 1
            rs_done = sCC.n
            if cut == "rs":
                return True

            if "dbg_rs" in dbg_aps and tag == "f1":
                @block.gpsimd
                def _(g):
                    g.wait_ge(sCC.h, rs_done)
                    g.dma_start(out=dbg_aps["dbg_rs"].rearrange("(b p) d -> p b d", p=128),
                                in_=rs_out.rearrange("(b p) d -> p b d", p=128)).then_inc(sG.h, 16)
                sG.n += 16
            if "dbg_pout" in dbg_aps and tag == "f1":
                @block.gpsimd
                def _(g):
                    g.wait_ge(sCC.h, rs_done)
                    for a in range(8):
                        g.dma_start(
                            out=dbg_aps["dbg_pout"][a * 1024:(a + 1) * 1024, :]
                                .rearrange("(b p) d -> p b d", p=128),
                            in_=pout[a * 1024:(a + 1) * 1024, :]
                                .rearrange("(b p) d -> p b d", p=128)).then_inc(sG.h, 16)
                sG.n += 8 * 16

            for j in range(NLO):
                @block.sync
                def _(s, j=j, rs_done=rs_done):
                    if j == 0:
                        s.wait_ge(sCC.h, rs_done)
                    else:
                        s.wait_ge(sV.h, sV.n)
                    s.dma_start(out=big[:, BIG_Z:BIG_Z + D],
                                in_=rs_out[j * 128:(j + 1) * 128, :]).then_inc(sS.h, 16)
                sS.n += 16
                ld_done = sS.n

                @block.vector
                def _(v, j=j, ld_done=ld_done):
                    v.wait_ge(sS.h, ld_done)
                    v.scalar_tensor_tensor(xs3(j), big[:, BIG_Z:BIG_Z + D], 0.5,
                                           xs3(j), ALU.mult, ALU.add)
                    v.drain()
                    v.nop().then_inc(sV.h, 1)
                sV.n += 1

        def emit_tail():
            @block.sync
            def _(s):
                s.wait_ge(sV.h, sV.n)
                s.wait_ge(sC.h, sC.n)
                s.wait_ge(sT.h, sT.n)
                s.wait_ge(sP.h, sP.n)
                s.wait_ge(sCC.h, sCC.n)
                s.wait_ge(sG.h, sG.n)
                s.dma_start(out=x_out.rearrange("(j p) d -> p j d", p=128),
                            in_=xs[:].rearrange("p (j d) -> p j d", d=D)).then_inc(sS.h, 16)
                sS.n += 16
                s.wait_ge(sS.h, sS.n)

        if emit_ffn("f1", cut=cut):
            emit_tail()
            return nc
        if nphases < 2:
            emit_tail()
            return nc

        p2_stats_done = emit_ln_stats(NT)
        p2_trl = {}
        p2_ev = {}
        for j in range(NT):
            zofs = BIG_Z if j % 2 == 0 else BIG_ST

            @block.scalar
            def _(s, j=j, zofs=zofs):
                if j == 0:
                    s.wait_ge(sV.h, p2_stats_done)
                if j >= 2:
                    s.wait_ge(sT.h, p2_trl[j - 2])
                s.activation(big[:, zofs:zofs + D], xs3(j), AF.Identity,
                             bias=cnt_sb[:, j:j + 1], scale=nM_sb[:, j:j + 1])
                s.drain()
                s.nop().then_inc(sC.h, 1)
            sC.n += 1
            z_done = sC.n

            @block.sync
            def _(s, j=j, z_done=z_done):
                s.wait_ge(sC.h, z_done)
                s.dma_start(out=xspill[j * 128:(j + 1) * 128, :], in_=xs3(j)).then_inc(sS.h, 16)
            sS.n += 16
            spill_done = sS.n

            for half in range(2):
                @block.tensor
                def _(t, j=j, half=half, z_done=z_done, spill_done=spill_done, zofs=zofs):
                    t.wait_ge(sC.h, z_done)
                    if half == 0:
                        t.wait_ge(sS.h, spill_done)
                    if j >= 1:
                        t.wait_ge(sV.h, p2_ev[(j - 1, half)])
                    bank = pb[4 + half]
                    for u in range(4):
                        a_ = half * 4 + u
                        instr = t.transpose(bank[:, u * 128:(u + 1) * 128],
                                            big[:, zofs + a_ * 128: zofs + (a_ + 1) * 128],
                                            ident[:])
                    instr.then_inc(sT.h, 1)
                sT.n += 1
                tr_done = sT.n

                @block.vector
                def _(v, j=j, half=half, tr_done=tr_done):
                    v.wait_ge(sT.h, tr_done)
                    v.tensor_copy(xs[:, j * D + half * 512: j * D + (half + 1) * 512],
                                  pb[4 + half][:])
                    v.drain()
                    v.nop().then_inc(sV.h, 1)
                sV.n += 1
                p2_ev[(j, half)] = sV.n
            p2_trl[j] = sT.n
        zt_done_v = sV.n

        prev_kdma = 0
        prev_vtdma = 0
        prev_vt_done = 0
        for o in range(DCH):
            @block.sync
            def _(s, o=o):
                if o >= 1:
                    s.wait_ge(sT.h, sT.n)
                for wi, wap in enumerate((wq_in, wk_in, wv_in)):
                    s.dma_start(
                        out=big[:, BIG_W1 + wi * 1024: BIG_W1 + (wi + 1) * 1024]
                            .rearrange("p (a w) -> p a w", w=128),
                        in_=wap[:, o * 128:(o + 1) * 128].rearrange("(a p) w -> p a w", p=128),
                    ).then_inc(sS.h, 16)
            sS.n += 48
            w_done = sS.n

            for tb in range(4):
                @block.tensor
                def _(t, o=o, tb=tb, w_done=w_done):
                    t.wait_ge(sS.h, w_done)
                    if o == 0 and tb == 0:
                        t.wait_ge(sV.h, zt_done_v)
                        t.wait_ge(sC.h, sC.n)
                    else:
                        t.wait_ge(sC.h, sC.n)
                    for wi in range(3):
                        for a in range(DCH):
                            instr = t.matmul(
                                pb[wi][:],
                                big[:, BIG_W1 + wi * 1024 + a * 128:
                                       BIG_W1 + wi * 1024 + (a + 1) * 128],
                                xs[:].rearrange("p (j d) -> p j d", d=D)
                                    [:, tb * 4:(tb + 1) * 4, a * 128:(a + 1) * 128],
                                start=(a == 0), stop=(a == DCH - 1))
                    instr.then_inc(sT.h, 1)
                sT.n += 1
                qkv_mm = sT.n

                @block.scalar
                def _(s, o=o, tb=tb, qkv_mm=qkv_mm, prev_kdma=prev_kdma,
                      prev_vt_done=prev_vt_done):
                    s.wait_ge(sT.h, qkv_mm)
                    s.wait_ge(sS.h, prev_kdma)
                    s.wait_ge(sT.h, prev_vt_done)
                    s.activation(qf[:, o * 2048 + tb * 512: o * 2048 + (tb + 1) * 512],
                                 pb[0][:], AF.Identity, bias=bqkv[:, o:o + 1])
                    s.activation(big[:, BIG_ST:BIG_ST + 512], pb[1][:], AF.Identity,
                                 bias=bqkv[:, 8 + o:8 + o + 1])
                    s.activation(big[:, BIG_ST + 512:BIG_ST + 1024], pb[2][:], AF.Identity,
                                 bias=bqkv[:, 16 + o:16 + o + 1])
                    s.drain()
                    s.nop().then_inc(sC.h, 1)
                sC.n += 1
                ev_done = sC.n

                @block.sync
                def _(s, o=o, tb=tb, ev_done=ev_done):
                    s.wait_ge(sC.h, ev_done)
                    kvag = kv_agin_lo if tb < 2 else kv_agin_hi
                    tb_ = tb % 2
                    s.dma_start(out=kvag[o * 128:(o + 1) * 128, tb_ * 512:(tb_ + 1) * 512],
                                in_=big[:, BIG_ST:BIG_ST + 512]).then_inc(sS.h, 16)
                sS.n += 16
                prev_kdma = sS.n

                @block.tensor
                def _(t, ev_done=ev_done, prev_vtdma=prev_vtdma):
                    t.wait_ge(sC.h, ev_done)
                    t.wait_ge(sV.h, sV.n)
                    for u in range(4):
                        instr = t.transpose(pb[3][:, u * 128:(u + 1) * 128],
                                            big[:, BIG_ST + 512 + u * 128:
                                                   BIG_ST + 512 + (u + 1) * 128],
                                            ident[:])
                    instr.then_inc(sT.h, 1)
                sT.n += 1
                prev_vt_done = sT.n

                @block.vector
                def _(v, prev_vt_done=prev_vt_done, prev_vtdma=prev_vtdma):
                    v.wait_ge(sT.h, prev_vt_done)
                    v.wait_ge(sS.h, prev_vtdma)
                    v.tensor_copy(pt_sb[:], pb[3][:])
                    v.drain()
                    v.nop().then_inc(sV.h, 1)
                sV.n += 1
                vtev = sV.n

                @block.sync
                def _(s, o=o, tb=tb, vtev=vtev):
                    s.wait_ge(sV.h, vtev)
                    kvag = kv_agin_lo if tb < 2 else kv_agin_hi
                    tb_ = tb % 2
                    s.dma_start(
                        out=kvag[1024 + tb_ * 512:1024 + (tb_ + 1) * 512,
                                 o * 128:(o + 1) * 128]
                            .rearrange("(u p) w -> p u w", p=128),
                        in_=pt_sb[:].rearrange("p (u w) -> p u w", w=128),
                    ).then_inc(sS.h, 16)
                sS.n += 16
                prev_vtdma = sS.n

        @block.gpsimd
        def _(g):
            g.wait_ge(sS.h, sS.n)
            g.collective_compute("AllGather", ALU.bypass,
                                 replica_groups=[list(range(NCORES))],
                                 ins=[kv_agin_lo], outs=[kv_all_lo]).then_inc(sCC.h, 1)
            g.collective_compute("AllGather", ALU.bypass,
                                 replica_groups=[list(range(NCORES))],
                                 ins=[kv_agin_hi], outs=[kv_all_hi]).then_inc(sCC.h, 1)
        sCC.n += 2
        kv_ag_lo = sCC.n - 1
        kv_ag = sCC.n

        if "dbg_q" in dbg_aps:
            @block.sync
            def _(s):
                s.wait_ge(sC.h, sC.n)
                s.dma_start(out=dbg_aps["dbg_q"], in_=qf[:]).then_inc(sS.h, 16)
                sS.n += 16

        if nphases < 3:
            emit_tail()
            return nc

        exp_done_at = {}
        mask_done_at = {}
        ptd_at = {}
        ptev_at = {}
        av_at = {}
        oupd_at = {}
        kvld_at = {}
        kb_last_vi = {}
        kb_of_vi = {}

        def emit_kv_load(kb):
            @block.sync
            def _(s, kb=kb):
                if kb <= 1:
                    s.wait_ge(sCC.h, kv_ag_lo)
                elif kb >= 2:
                    s.wait_ge(sT.h, av_at[kb_last_vi[kb - 2]])
                if kb >= 16 and kb <= 17:
                    s.wait_ge(sCC.h, kv_ag)
                base = 8192 * (kb % 2)
                for i in range(4):
                    gtile = 4 * kb + i
                    r, lj = gtile & 7, gtile >> 3
                    kva = kv_all_lo if lj < 8 else kv_all_hi
                    lj_ = lj % 8
                    s.dma_start(
                        out=xs[:, base:base + 4096].rearrange("p (a w) -> p a w", w=512)
                            [:, :, i * 128:(i + 1) * 128],
                        in_=kva[2048 * r:2048 * r + 1024, 128 * lj_:128 * (lj_ + 1)]
                            .rearrange("(a p) w -> p a w", p=128),
                    ).then_inc(sS.h, 16)
                    s.dma_start(
                        out=xs[:, base + 4096 + i * 1024: base + 4096 + (i + 1) * 1024],
                        in_=kva[2048 * r + 1024 + 128 * lj_:
                                2048 * r + 1024 + 128 * (lj_ + 1), :],
                    ).then_inc(sS.h, 16)
            sS.n += 8 * 16
            kvld_at[kb] = sS.n

        def emit_scores(vi_, j, kb):
            sbk = vi_ % 3
            base = 8192 * (kb % 2)

            @block.tensor
            def _(t, vi_=vi_, j=j, kb=kb, sbk=sbk, base=base):
                if vi_ == 0 or kb_of_vi.get(vi_ - 1) != kb:
                    t.wait_ge(sS.h, kvld_at[kb])
                if vi_ - 3 in exp_done_at:
                    t.wait_ge(sC.h, exp_done_at[vi_ - 3])
                for a in range(DCH):
                    instr = t.matmul(
                        pb[sbk][:],
                        qf[:, a * 2048 + j * 128: a * 2048 + (j + 1) * 128],
                        xs[:, base:base + 4096].rearrange("p (a w) -> p a w", w=512)[:, a, :],
                        start=(a == 0), stop=(a == DCH - 1))
                instr.then_inc(sT.h, 1)
            sT.n += 1
            return sT.n

        def emit_rest(vi_, j, kb, sc_at):
            first = (kb == 0)
            sbk = vi_ % 3
            tbk = 3
            ab0 = 4 + 2 * (vi_ % 2)
            ab1 = ab0 + 1
            pofs = 512 * (vi_ % 3)

            if first:
                @block.vector
                def _(v, j=j, sbk=sbk, sc_at=sc_at):
                    v.wait_ge(sT.h, sc_at)
                    v.tensor_reduce(sm1[:], pb[sbk][:], AX.X, ALU.max)
                    v.drain()
                    v.tensor_scalar(sm1[:], sm1[:], INV_SQ_D, None, ALU.mult)
                    v.drain()
                    v.tensor_copy(M_sb[:, j:j + 1], sm1[:])
                    v.tensor_scalar(nM_sb[:, j:j + 1], sm1[:], -1.0, None, ALU.mult)
                    v.drain()
                    v.nop().then_inc(sV.h, 1)
                sV.n += 1
            stats_at = sV.n

            @block.scalar
            def _(s, vi_=vi_, j=j, sbk=sbk, pofs=pofs, sc_at=sc_at,
                  first=first, stats_at=stats_at):
                s.wait_ge(sT.h, sc_at)
                if first:
                    s.wait_ge(sV.h, stats_at)
                if vi_ - 3 in ptd_at:
                    s.wait_ge(sT.h, ptd_at[vi_ - 3])
                s.activation(pm_sb[:, pofs:pofs + 512], pb[sbk][:], AF.Exp,
                             scale=INV_SQ_D, bias=nM_sb[:, j:j + 1])
                s.drain()
                s.nop().then_inc(sC.h, 1)
            sC.n += 1
            exp_done_at[vi_] = sC.n

            @block.vector
            def _(v, vi_=vi_, j=j, first=first, pofs=pofs):
                v.wait_ge(sC.h, exp_done_at[vi_])
                v.scalar_tensor_tensor(pm_sb[:, pofs:pofs + 512], iota_f[:],
                                       lim_sb[:, vi_:vi_ + 1],
                                       pm_sb[:, pofs:pofs + 512], ALU.is_le, ALU.mult)
                v.drain()
                v.tensor_reduce(sm5[:], pm_sb[:, pofs:pofs + 512], AX.X, ALU.add)
                v.drain()
                if first:
                    v.tensor_copy(L_sb[:, j:j + 1], sm5[:])
                else:
                    v.tensor_tensor(L_sb[:, j:j + 1], L_sb[:, j:j + 1], sm5[:], ALU.add)
                v.drain()
                v.nop().then_inc(sV.h, 1)
            sV.n += 1
            mask_done_at[vi_] = sV.n

            @block.tensor
            def _(t, vi_=vi_, tbk=tbk, pofs=pofs):
                t.wait_ge(sV.h, mask_done_at[vi_])
                if vi_ - 1 in ptev_at:
                    t.wait_ge(sV.h, ptev_at[vi_ - 1])
                for u in range(4):
                    instr = t.transpose(pb[tbk][:, u * 128:(u + 1) * 128],
                                        pm_sb[:, pofs + u * 128: pofs + (u + 1) * 128],
                                        ident[:])
                instr.then_inc(sT.h, 1)
            sT.n += 1
            ptd_at[vi_] = sT.n

            @block.vector
            def _(v, vi_=vi_, tbk=tbk):
                v.wait_ge(sT.h, ptd_at[vi_])
                if vi_ - 1 in av_at:
                    v.wait_ge(sT.h, av_at[vi_ - 1])
                v.tensor_copy(pt_sb[:], pb[tbk][:])
                v.drain()
                v.nop().then_inc(sV.h, 1)
            sV.n += 1
            ptev_at[vi_] = sV.n

            @block.tensor
            def _(t, vi_=vi_, kb=kb, ab0=ab0, ab1=ab1, pofs=pofs):
                t.wait_ge(sV.h, ptev_at[vi_])
                if vi_ - 2 in oupd_at:
                    t.wait_ge(sV.h, oupd_at[vi_ - 2])
                base = 8192 * (kb % 2)
                for h_ in range(2):
                    for u in range(4):
                        instr = t.matmul(
                            pb[ab0 + h_][:],
                            pt_sb[:, u * 128:(u + 1) * 128],
                            xs[:, base + 4096 + u * 1024 + h_ * 512:
                                   base + 4096 + u * 1024 + (h_ + 1) * 512],
                            start=(u == 0), stop=(u == 3))
                instr.then_inc(sT.h, 1)
            sT.n += 1
            av_at[vi_] = sT.n

            @block.vector
            def _(v, vi_=vi_, j=j, first=first, ab0=ab0, ab1=ab1):
                v.wait_ge(sT.h, av_at[vi_])
                Oj0 = big[:, j * 1024: j * 1024 + 512]
                Oj1 = big[:, j * 1024 + 512: (j + 1) * 1024]
                if first:
                    v.tensor_copy(Oj0, pb[ab0][:])
                    v.tensor_copy(Oj1, pb[ab1][:])
                else:
                    v.tensor_tensor(Oj0, Oj0, pb[ab0][:], ALU.add)
                    v.tensor_tensor(Oj1, Oj1, pb[ab1][:], ALU.add)
                v.drain()
                v.nop().then_inc(sV.h, 1)
            sV.n += 1
            oupd_at[vi_] = sV.n

        visits = []
        for kb in range(NKB):
            for j in range(kb // 2, NT):
                visits.append((kb, j))
                kb_of_vi[len(visits) - 1] = kb
                kb_last_vi[kb] = len(visits) - 1

        sc_ats = {}
        emit_kv_load(0)
        emit_kv_load(1)
        NV_ = len(visits)
        rest_next = [0]

        def emit_rest_one():
            rn = rest_next[0]
            pkb, pj = visits[rn]
            emit_rest(rn, pj, pkb, sc_ats[rn])
            if rn == kb_last_vi[pkb] and pkb + 2 < NKB:
                emit_kv_load(pkb + 2)
            rest_next[0] = rn + 1

        for vi_, (kb, j) in enumerate(visits):
            while kb not in kvld_at:
                emit_rest_one()
            sc_ats[vi_] = emit_scores(vi_, j, kb)
            while rest_next[0] <= vi_ - 2:
                emit_rest_one()
        while rest_next[0] < NV_:
            emit_rest_one()


        if "dbg_ml" in dbg_aps:
            @block.sync
            def _(s):
                s.wait_ge(sV.h, sV.n)
                s.dma_start(out=dbg_aps["dbg_ml"][:, 0:16], in_=M_sb[:]).then_inc(sS.h, 16)
                s.dma_start(out=dbg_aps["dbg_ml"][:, 16:32], in_=L_sb[:]).then_inc(sS.h, 16)
                sS.n += 32

        if nphases < 4:
            emit_tail()
            return nc

        @block.vector
        def _(v):
            v.wait_ge(sV.h, sV.n)
            v.memset(ones1[:], 1.0)
            v.tensor_reduce(sm1[:], M_sb[:], AX.X, ALU.max)
            v.drain()
            v.nop().then_inc(sV.h, 1)
        sV.n += 1
        mrow_done = sV.n

        @block.tensor
        def _(t, mrow_done=mrow_done):
            t.wait_ge(sV.h, mrow_done)
            t.transpose(pb[0][0:1, 0:128], sm1[:], ident[:]).then_inc(sT.h, 1)
        sT.n += 1
        mt_done = sT.n

        @block.vector
        def _(v, mt_done=mt_done):
            v.wait_ge(sT.h, mt_done)
            v.tensor_copy(pm_sb[0:1, 1024:1152], pb[0][0:1, 0:128])
            v.drain()
            v.tensor_reduce(mzs[:, :], pm_sb[0:1, 1024:1152], AX.X, ALU.max)
            v.drain()
            v.tensor_scalar(mzs[:, :], mzs[:, :], -1.0, None, ALU.mult)
            v.drain()
            v.nop().then_inc(sV.h, 1)
        sV.n += 1
        mloc_done = sV.n

        @block.sync
        def _(s, mloc_done=mloc_done):
            s.wait_ge(sV.h, mloc_done)
            s.dma_start(out=bscr[:, 0:1], in_=mzs[:, 0:1]).then_inc(sS.h, 16)
            s.wait_ge(sS.h, sS.n + 16)
            s.dma_start(out=sm3[:], in_=bscr[:, 0:1].to_broadcast((128, 1))).then_inc(sS.h, 16)
        sS.n += 32
        nm_bcast = sS.n

        @block.scalar
        def _(s, nm_bcast=nm_bcast):
            s.wait_ge(sS.h, nm_bcast)
            s.activation(pm_sb[:, 1024:1024 + NT], M_sb[:], AF.Exp, bias=sm3[:])
            s.drain()
            s.nop().then_inc(sC.h, 1)
        sC.n += 1
        e16_done = sC.n

        @block.vector
        def _(v, e16_done=e16_done):
            v.wait_ge(sC.h, e16_done)
            v.tensor_tensor(pm_sb[:, 1024:1024 + NT], pm_sb[:, 1024:1024 + NT], L_sb[:], ALU.mult)
            v.drain()
            v.tensor_reduce(sm1[:], pm_sb[:, 1024:1024 + NT], AX.X, ALU.add)
            v.drain()
            v.nop().then_inc(sV.h, 1)
        sV.n += 1
        lz_done = sV.n

        @block.tensor
        def _(t, lz_done=lz_done):
            t.wait_ge(sV.h, lz_done)
            t.matmul(pb[1][0:1, 0:1], sm1[:], ones1[:], start=True, stop=True).then_inc(sT.h, 1)
        sT.n += 1
        zl_done = sT.n

        @block.vector
        def _(v, zl_done=zl_done):
            v.wait_ge(sT.h, zl_done)
            v.tensor_scalar(mzrow[:, 0:1], mzs[:, 0:1], -1.0, None, ALU.mult)
            v.tensor_copy(mzrow[:, 1:2], pb[1][0:1, 0:1])
            v.drain()
            v.nop().then_inc(sV.h, 1)
        sV.n += 1
        mzst = sV.n

        @block.sync
        def _(s, mzst=mzst):
            s.wait_ge(sV.h, mzst)
            s.dma_start(out=mz_in, in_=mzrow[:, 0:2]).then_inc(sS.h, 16)
        sS.n += 16
        mzd = sS.n

        @block.gpsimd
        def _(g, mzd=mzd):
            g.wait_ge(sS.h, mzd)
            g.collective_compute("AllGather", ALU.bypass,
                                 replica_groups=[list(range(NCORES))],
                                 ins=[mz_in], outs=[mz_all]).then_inc(sCC.h, 1)
        sCC.n += 1
        mzag = sCC.n

        @block.sync
        def _(s, mzag=mzag):
            s.wait_ge(sCC.h, mzag)
            s.dma_start(out=mzrow[:, :], in_=mz_all.rearrange("(p a) b -> p (a b)", p=1)).then_inc(sS.h, 16)
        sS.n += 16
        mzld = sS.n

        @block.vector
        def _(v, mzld=mzld):
            v.wait_ge(sS.h, mzld)
            v.tensor_reduce(mzred[:, :], mzrow[:, :].rearrange("p (a b) -> p b a", b=2),
                            AX.X, ALU.max)
            v.drain()
            v.tensor_scalar(mzs[:, :], mzred[:, 0:1], -1.0, None, ALU.mult)
            v.drain()
            v.nop().then_inc(sV.h, 1)
        sV.n += 1
        mstar = sV.n

        @block.scalar
        def _(s, mstar=mstar):
            s.wait_ge(sV.h, mstar)
            s.activation(mz8[:, :],
                         mzrow[:, :].rearrange("p (a b) -> p a b", b=2)[:, :, 0],
                         AF.Exp, bias=mzs[:, :])
            s.drain()
            s.nop().then_inc(sC.h, 1)
        sC.n += 1
        ez_done = sC.n

        @block.vector
        def _(v, ez_done=ez_done):
            v.wait_ge(sC.h, ez_done)
            v.tensor_tensor(mz8[:, :], mz8[:, :],
                            mzrow[:, :].rearrange("p (a b) -> p a b", b=2)[:, :, 1],
                            ALU.mult)
            v.drain()
            v.tensor_reduce(mzred[:, 1:2], mz8[:, :], AX.X, ALU.add)
            v.drain()
            v.reciprocal(mzred[:, 1:2], mzred[:, 1:2])
            v.drain()
            v.tensor_copy(mzrow[:, 2:3], mzs[:, 0:1])
            v.tensor_copy(mzrow[:, 3:4], mzred[:, 1:2])
            v.drain()
            v.nop().then_inc(sV.h, 1)
        sV.n += 1
        invz_done = sV.n

        @block.sync
        def _(s, invz_done=invz_done):
            s.wait_ge(sV.h, invz_done)
            s.dma_start(out=bscr, in_=mzrow[:, 2:4]).then_inc(sS.h, 16)
            s.wait_ge(sS.h, sS.n + 16)
            s.dma_start(out=bc2[:], in_=bscr.to_broadcast((128, 2))).then_inc(sS.h, 16)
        sS.n += 32
        bc_done = sS.n

        @block.scalar
        def _(s, bc_done=bc_done):
            s.wait_ge(sS.h, bc_done)
            s.activation(f_sb[:], M_sb[:], AF.Exp, bias=bc2[:, 0:1])
            s.drain()
            s.nop().then_inc(sC.h, 1)
        sC.n += 1
        f_exp = sC.n

        @block.vector
        def _(v, f_exp=f_exp):
            v.wait_ge(sC.h, f_exp)
            v.tensor_scalar(f_sb[:], f_sb[:], bc2[:, 1:2], None, ALU.mult)
            v.drain()
            v.nop().then_inc(sV.h, 1)
        sV.n += 1
        f_done = sV.n

        for j in range(NT):
            @block.sync
            def _(s, j=j, f_done=f_done):
                if j == 0:
                    s.wait_ge(sV.h, f_done)
                    s.wait_ge(sT.h, sT.n)
                s.dma_start(out=xs3(j), in_=xspill[j * 128:(j + 1) * 128, :]).then_inc(sS.h, 16)
            sS.n += 16
            xld = sS.n

            @block.vector
            def _(v, j=j, xld=xld):
                v.wait_ge(sS.h, xld)
                v.scalar_tensor_tensor(xs3(j), big[:, j * 1024:(j + 1) * 1024],
                                       f_sb[:, j:j + 1], xs3(j), ALU.mult, ALU.add)
                v.drain()
                v.nop().then_inc(sV.h, 1)
            sV.n += 1

        if nphases < 5:
            emit_tail()
            return nc

        emit_ffn("f2")
        emit_tail()
    return nc


def _limits_for_core(c):
    lim = np.full((LIMCOLS,), 1e4, np.float32)
    for vi, (kb, j) in enumerate(VISITS):
        lim[vi] = float(1024 * j + 128 * c - 512 * kb)
    return lim


def _shard_rows(c):
    tiles = [8 * j + c for j in range(NT)]
    return np.concatenate([np.arange(g * 128, (g + 1) * 128) for g in tiles])


def _pad_cw(cw):
    cwp = np.zeros((8192, 64), np.float32)
    cwp[:, 0:8] = np.asarray(cw, np.float32)
    t = np.arange(8192)
    cwp[:, 9] = t.astype(np.float32)
    return cwp


def _ptab():
    t = np.arange(8192)
    p = np.zeros((8192, 64), np.int32)
    p[:, 0] = (((t >> 7) & 7) << 10 | (t >> 10) << 7 | (t & 127)).astype(np.int32)
    return p


def make_in_maps(L0_attn_x, L0_attn_Wq, L0_attn_bq, L0_attn_Wk, L0_attn_bk,
                 L0_attn_Wv, L0_attn_bv, L0_ffn_combine_weights, L0_ffn_top_idx,
                 L0_ffn_W1, L0_ffn_b1, L0_ffn_W2, L0_ffn_b2):
    x = np.ascontiguousarray(np.asarray(L0_attn_x, np.float32).reshape(T, D))
    maps = []
    for c in range(NCORES):
        oh = np.zeros(64, np.float32)
        oh[c] = 1.0
        maps.append({
            "x_in": np.ascontiguousarray(x[_shard_rows(c)]),
            "wq": np.ascontiguousarray(np.asarray(L0_attn_Wq, np.float32)),
            "wk": np.ascontiguousarray(np.asarray(L0_attn_Wk, np.float32)),
            "wv": np.ascontiguousarray(np.asarray(L0_attn_Wv, np.float32)),
            "bq": np.ascontiguousarray(np.asarray(L0_attn_bq, np.float32)),
            "bk": np.ascontiguousarray(np.asarray(L0_attn_bk, np.float32)),
            "bv": np.ascontiguousarray(np.asarray(L0_attn_bv, np.float32)),
            "w1": np.ascontiguousarray(np.asarray(L0_ffn_W1, np.float32)[c]),
            "b1": np.ascontiguousarray(np.asarray(L0_ffn_b1, np.float32)[c]),
            "w2": np.ascontiguousarray(np.asarray(L0_ffn_W2, np.float32)[c]),
            "b2": np.ascontiguousarray(np.asarray(L0_ffn_b2, np.float32)[c]),
            "cw": _pad_cw(L0_ffn_combine_weights),
            "ptab": _ptab(),
            "oh128": np.tile(oh, 16),
            "idxc": np.ascontiguousarray(np.asarray(L0_ffn_top_idx, np.int32)[:, c]),
            "limits": _limits_for_core(c),
        })
    return maps


def kernel(**inputs):
    in_maps = make_in_maps(**inputs)
    nc = build_nc()
    mybir.codegen_inst_isa_subclasses(nc)
    res = run_bass_kernel_spmd(nc, in_maps, core_ids=list(range(NCORES)))
    out = np.empty((T, D), np.float32)
    for c in range(NCORES):
        out[_shard_rows(c)] = res.results[c]["x_out"]
    return out.reshape(1, T, D)


# version 11 (final)
# speedup vs baseline: 1.0043x; 1.0043x over previous
import numpy as np
from contextlib import ExitStack

import bass_rust
import concourse.bass as bass
import concourse.mybir as mybir
from concourse.bass import IndirectOffsetOnAxis
from concourse import library_config
from concourse.bass_utils import run_bass_kernel_spmd

F32 = mybir.dt.float32
I32 = mybir.dt.int32
I16 = mybir.dt.int16
U32 = mybir.dt.uint32
AF = mybir.ActivationFunctionType
ALU = mybir.AluOpType
AX = mybir.AxisListType
RED = bass_rust.ReduceOp

NCORES = 8
T = 16384
D = 1024
DCH = 8
FF = 4096
FFCH = 32
CSL = 2048
NT = 16
NLO = 8
NKB = 32
SB = 256
NSB = CSL // SB
INV_SQ_D = 1.0 / 32.0
EPS = 1e-5

VISITS = [(kb, j) for kb in range(NKB) for j in range(kb // 2, NT)]
NVIS = len(VISITS)
LIMCOLS = 272


def build_nc(nphases=6, dbg=(), cut=None, ablate=()):
    nc = bass.Bass("TRN2", target_bir_lowering=False, detect_race_conditions=False)

    x_in = nc.dram_tensor("x_in", [2048, D], F32, kind="ExternalInput").ap()
    wq_in = nc.dram_tensor("wq", [D, D], F32, kind="ExternalInput").ap()
    wk_in = nc.dram_tensor("wk", [D, D], F32, kind="ExternalInput").ap()
    wv_in = nc.dram_tensor("wv", [D, D], F32, kind="ExternalInput").ap()
    bq_in = nc.dram_tensor("bq", [D], F32, kind="ExternalInput").ap()
    bk_in = nc.dram_tensor("bk", [D], F32, kind="ExternalInput").ap()
    bv_in = nc.dram_tensor("bv", [D], F32, kind="ExternalInput").ap()
    w1_in = nc.dram_tensor("w1", [D, FF], F32, kind="ExternalInput").ap()
    b1_in = nc.dram_tensor("b1", [FF], F32, kind="ExternalInput").ap()
    w2_in = nc.dram_tensor("w2", [FF, D], F32, kind="ExternalInput").ap()
    b2_in = nc.dram_tensor("b2", [D], F32, kind="ExternalInput").ap()
    cw_in = nc.dram_tensor("cw", [8192, 64], F32, kind="ExternalInput").ap()
    oh_in = nc.dram_tensor("oh128", [1024], F32, kind="ExternalInput").ap()
    idx_in = nc.dram_tensor("idxc", [CSL], I32, kind="ExternalInput").ap()
    lim_in = nc.dram_tensor("limits", [LIMCOLS], F32, kind="ExternalInput").ap()
    pt_in = nc.dram_tensor("ptab", [8192, 64], I32, kind="ExternalInput").ap()

    x_out = nc.dram_tensor("x_out", [2048, D], F32, kind="ExternalOutput").ap()

    y_agin = nc.dram_tensor("y_agin", [1024, D], F32).ap()
    y_all = nc.dram_tensor("y_all", [8192, D], F32).ap()
    kv_agin_lo = nc.dram_tensor("kv_agin_lo", [2048, 1024], F32).ap()
    kv_agin_hi = nc.dram_tensor("kv_agin_hi", [2048, 1024], F32).ap()
    kv_all_lo = nc.dram_tensor("kv_all_lo", [T, 1024], F32).ap()
    kv_all_hi = nc.dram_tensor("kv_all_hi", [T, 1024], F32).ap()
    pout = nc.dram_tensor("pout", [8192, D], F32).ap()
    rs_out = nc.dram_tensor("rs_out", [1024, D], F32).ap()
    xspill = nc.dram_tensor("xspill", [2048, D], F32).ap()
    mz_in = nc.dram_tensor("mz_in", [1, 2], F32).ap()
    mz_all = nc.dram_tensor("mz_all", [8, 2], F32).ap()
    bscr = nc.dram_tensor("bscr", [1, 2], F32).ap()
    pspill = nc.dram_tensor("pspill", [2, CSL], F32).ap()

    dbg_aps = {}
    for name, shape in (
        ("dbg_yall", [8192, D]), ("dbg_rs", [1024, D]), ("dbg_pout", [8192, D]),
        ("dbg_kall", [8 * D, 2048]), ("dbg_vall", [T, D]), ("dbg_q", [128, 16384]),
        ("dbg_ml", [128, 32]), ("dbg_cnt", [128, 48]),
        ("dbg_h", [128, 16384]), ("dbg_gt", [128, 16384]),
    ):
        if name in dbg:
            dbg_aps[name] = nc.dram_tensor(name, shape, F32, kind="ExternalOutput").ap()

    st = ExitStack()
    with st:
        e = st.enter_context

        xs = e(nc.sbuf_tensor("xs", [128, NT * D], F32))
        big = e(nc.sbuf_tensor("big", [128, 16384], F32))
        qf = e(nc.sbuf_tensor("qf", [128, 16384], F32))

        pm_sb = e(nc.sbuf_tensor("pm_sb", [128, 1536], F32))
        pt_sb = e(nc.sbuf_tensor("pt_sb", [128, 512], F32))
        iota_f = e(nc.sbuf_tensor("iota_f", [128, 512], F32))
        lim_sb = e(nc.sbuf_tensor("lim_sb", [128, LIMCOLS], F32))
        M_sb = e(nc.sbuf_tensor("M_sb", [128, NT], F32))
        nM_sb = e(nc.sbuf_tensor("nM_sb", [128, NT], F32))
        L_sb = e(nc.sbuf_tensor("L_sb", [128, NT], F32))
        cnt_sb = e(nc.sbuf_tensor("cnt_sb", [128, NT], F32))
        w_sb = e(nc.sbuf_tensor("w_sb", [128, NT], F32))
        scl_sb = e(nc.sbuf_tensor("scl_sb", [128, NT], F32))
        sm1 = e(nc.sbuf_tensor("sm1", [128, 1], F32))
        sm2 = e(nc.sbuf_tensor("sm2", [128, 1], F32))
        sm3 = e(nc.sbuf_tensor("sm3", [128, 1], F32))
        sm4 = e(nc.sbuf_tensor("sm4", [128, 1], F32))
        sm5 = e(nc.sbuf_tensor("sm5", [128, 1], F32))
        sm6 = e(nc.sbuf_tensor("sm6", [128, 1], F32))
        sm7 = e(nc.sbuf_tensor("sm7", [128, 1], F32))
        sm8 = e(nc.sbuf_tensor("sm8", [128, 1], F32))
        b1s = e(nc.sbuf_tensor("b1s", [128, FFCH], F32))
        ident = e(nc.sbuf_tensor("ident", [128, 128], F32))
        ones1 = e(nc.sbuf_tensor("ones1", [128, 1], F32))
        bc2 = e(nc.sbuf_tensor("bc2", [128, 2], F32))

        pb = [e(nc.psum_tensor(f"pb{i}", [128, 512], F32)) for i in range(8)]

        class Sem:
            def __init__(self, name):
                self.h = e(nc.semaphore(name))
                self.n = 0

        sS = Sem("sS")
        sG = Sem("sG")
        sCC = Sem("sCC")
        sT = Sem("sT")
        sV = Sem("sV")
        sC = Sem("sC")
        sP = Sem("sP")

        bqkv = pm_sb[:, 1056:1080]
        idx16p = pt_sb[:, 0:64].bitcast(I16)
        ixu = pt_sb[:, 128:144].bitcast(U32)
        idx16r = pt_sb[:, 64:128].bitcast(I16)
        f_sb = pm_sb[:, 48:64]
        ssq = sm5
        mzrow = pm_sb[0:1, 0:16]
        mzred = pm_sb[0:1, 16:18]
        mz8 = pm_sb[0:1, 24:32]
        mzs = pm_sb[0:1, 32:33]
        block = e(nc.Block())

        def xs3(j):
            return xs[:, j * D:(j + 1) * D]

        BIG_H = 0
        BIG_W1 = 8192
        BIG_Z = 11264
        BIG_SQ = 12288
        BIG_ST = 13312
        BIG_B2 = 14336
        BIG_ONE = 15360

        QF_GT = 0
        QF_G = 4096
        QF_SC = 8192
        QF_CMP = 10240
        QF_IB = 12288
        QF_WA = 14336

        @block.sync
        def _(s):
            s.dma_start(out=xs[:].rearrange("p (j d) -> p j d", d=D),
                        in_=x_in.rearrange("(j p) d -> p j d", p=128)).then_inc(sS.h, 16)
            s.dma_start(out=bqkv[:, 0:8], in_=bq_in.rearrange("(o p) -> p o", p=128), allow_slow_non_contiguous=True).then_inc(sS.h, 16)
            s.dma_start(out=bqkv[:, 8:16], in_=bk_in.rearrange("(o p) -> p o", p=128), allow_slow_non_contiguous=True).then_inc(sS.h, 16)
            s.dma_start(out=bqkv[:, 16:24], in_=bv_in.rearrange("(o p) -> p o", p=128), allow_slow_non_contiguous=True).then_inc(sS.h, 16)
            s.dma_start(out=b1s[:], in_=b1_in.rearrange("(o p) -> p o", p=128), allow_slow_non_contiguous=True).then_inc(sS.h, 16)
            s.dma_start(out=lim_sb[:], in_=lim_in.rearrange("(p d) -> p d", p=1)
                        .to_broadcast((128, LIMCOLS))).then_inc(sS.h, 16)
            s.dma_start(out=qf[:, 0:1024], in_=oh_in.rearrange("(p d) -> p d", p=1)
                        .to_broadcast((128, 1024))).then_inc(sS.h, 16)
        sS.n += 7 * 16
        P0_SYNC = sS.n

        @block.gpsimd
        def _(g):
            g.iota(pm_sb[:, 0:512].bitcast(I32), pattern=[[1, 512]], base=0, channel_multiplier=-1)
            g.nop().then_inc(sP.h, 1)
            g.load_library(library_config.mlp)
            g.dma_start(out=idx16r[0:16, :], in_=idx_in.rearrange("(s p) -> p s", p=16),
                        allow_slow_non_contiguous=True).then_inc(sG.h, 16)
            g.wait_ge(sG.h, sG.n + 16)
            for gi in range(1, 8):
                g.dma_start(out=idx16r[16 * gi:16 * (gi + 1), :],
                            in_=idx16r[0:16, :]).then_inc(sG.h, 16)
            g.wait_ge(sG.h, sG.n + 128)
            g.dma_gather(out_ap=qf[:, QF_WA:QF_WA + 1024].rearrange("p (b w) -> p b w", w=64),
                         in_ap=cw_in, idxs_ap=idx16r[:], num_idxs=CSL,
                         num_idxs_reg=CSL, elem_size=64,
                         single_packet=False).then_inc(sG.h, 16)
            g.dma_gather(out_ap=qf[:, QF_IB:QF_IB + 1024].bitcast(I32)
                         .rearrange("p (b w) -> p b w", w=64),
                         in_ap=pt_in, idxs_ap=idx16r[:], num_idxs=CSL,
                         num_idxs_reg=CSL, elem_size=64,
                         single_packet=False).then_inc(sG.h, 16)
            g.nop().then_inc(sP.h, 1)
        sG.n += 10 * 16
        sP.n += 2
        P0_GP = sP.n
        P0_G2 = sG.n

        @block.vector
        def _(v):
            v.wait_ge(sP.h, P0_GP)
            v.wait_ge(sS.h, P0_SYNC)
            v.wait_ge(sG.h, P0_G2)
            v.tensor_copy(iota_f[:], pm_sb[:, 0:512].bitcast(I32))
            v.drain()
            v.tensor_scalar(ident[:], iota_f[:, 0:128], 0.0, None, ALU.is_equal)
            v.tensor_copy(w_sb[:], qf[:, QF_WA:QF_WA + 1024]
                          .rearrange("p (b w) -> p b w", w=64)[:, :, 9])
            v.drain()
            v.nop().then_inc(sV.h, 1)
        sV.n += 1
        P0_VEC = sV.n

        @block.gpsimd
        def _(g):
            g.wait_ge(sV.h, P0_VEC)
            g.dma_start(out=pspill[0:1, :].bitcast(I32).rearrange("q (b p) -> (q p) b", p=128),
                        in_=qf[:, QF_IB:QF_IB + 1024].bitcast(I32)
                        .rearrange("p (b w) -> p b w", w=64)[:, :, 0],
                        allow_slow_non_contiguous=True).then_inc(sG.h, 16)
            g.dma_start(out=pspill[1:2, :].rearrange("q (b p) -> (q p) b", p=128),
                        in_=w_sb[:], allow_slow_non_contiguous=True).then_inc(sG.h, 16)
            g.wait_ge(sG.h, sG.n + 32)
            g.dma_start(out=ixu[:],
                        in_=pspill[0:1, :].bitcast(I32).rearrange("q (b p) -> (q p) b", p=128),
                        allow_slow_non_contiguous=True).then_inc(sG.h, 16)
            g.dma_start(out=idx16p[0:16, :],
                        in_=pspill[0:1, :].bitcast(I32).rearrange("q (s p) -> (q p) s", p=16),
                        allow_slow_non_contiguous=True).then_inc(sG.h, 16)
            g.dma_start(out=qf[:, QF_CMP:QF_CMP + CSL],
                        in_=pspill[1:2, :].to_broadcast((128, CSL))).then_inc(sG.h, 16)
            g.wait_ge(sG.h, sG.n + 80)
            for gi in range(1, 8):
                g.dma_start(out=idx16p[16 * gi:16 * (gi + 1), :],
                            in_=idx16p[0:16, :]).then_inc(sG.h, 16)
            g.nop().then_inc(sP.h, 1)
        sG.n += 12 * 16
        sP.n += 1
        P0_G3 = sG.n
        P0_GP2 = sP.n

        @block.vector
        def _(v):
            v.wait_ge(sG.h, P0_G3)
            v.wait_ge(sP.h, P0_GP2)
            for m in range(NT):
                v.tensor_scalar(qf[:, QF_IB:QF_IB + CSL], qf[:, QF_CMP:QF_CMP + CSL],
                                w_sb[:, m:m + 1], None, ALU.subtract)
                v.drain()
                v.tensor_scalar(qf[:, QF_IB:QF_IB + CSL], qf[:, QF_IB:QF_IB + CSL],
                                0.0, None, ALU.is_equal)
                v.drain()
                v.tensor_reduce(cnt_sb[:, m:m + 1], qf[:, QF_IB:QF_IB + CSL],
                                AX.X, ALU.add)
            v.tensor_tensor(qf[:, QF_WA:QF_WA + 1024], qf[:, QF_WA:QF_WA + 1024],
                            qf[:, 0:1024], ALU.mult)
            v.drain()
            for m in range(NT):
                v.tensor_reduce(w_sb[:, m:m + 1], qf[:, QF_WA + 64 * m: QF_WA + 64 * (m + 1)],
                                AX.X, ALU.add)
            v.drain()
            v.tensor_tensor(scl_sb[:], w_sb[:], cnt_sb[:], ALU.mult)
            v.drain()
            v.nop().then_inc(sV.h, 1)
        sV.n += 1
        P0_VEC2 = sV.n

        if "dbg_cnt" in dbg_aps:
            @block.sync
            def _(s):
                s.wait_ge(sV.h, P0_VEC2)
                s.dma_start(out=dbg_aps["dbg_cnt"][:, 0:16], in_=cnt_sb[:]).then_inc(sS.h, 16)
                s.dma_start(out=dbg_aps["dbg_cnt"][:, 16:32], in_=w_sb[:]).then_inc(sS.h, 16)
                s.dma_start(out=dbg_aps["dbg_cnt"][:, 32:48],
                            in_=ixu[:].bitcast(F32)).then_inc(sS.h, 16)
                sS.n += 48

        def emit_ln(j, dst_off):
            @block.scalar
            def _(s):
                s.activation(big[:, BIG_SQ:BIG_SQ + D], xs3(j), AF.Square)
                s.drain()
                s.nop().then_inc(sC.h, 1)
            sC.n += 1
            sq_done = sC.n

            @block.vector
            def _(v):
                v.wait_ge(sC.h, sq_done)
                v.tensor_reduce(ssq[:], big[:, BIG_SQ:BIG_SQ + D], AX.X, ALU.add)
                v.tensor_reduce(sm6[:], xs3(j), AX.X, ALU.add)
                v.drain()
                v.tensor_scalar(sm6[:], sm6[:], 1.0 / D, None, ALU.mult)
                v.tensor_scalar(sm8[:], ssq[:], 1.0 / D, None, ALU.mult)
                v.drain()
                v.tensor_tensor(sm7[:], sm6[:], sm6[:], ALU.mult)
                v.drain()
                v.tensor_tensor(sm8[:], sm8[:], sm7[:], ALU.subtract)
                v.drain()
                v.tensor_scalar(sm8[:], sm8[:], EPS, None, ALU.add)
                v.drain()
                v.nop().then_inc(sV.h, 1)
            sV.n += 1
            var_done = sV.n

            @block.scalar
            def _(s):
                s.wait_ge(sV.h, var_done)
                s.activation(sm7[:], sm8[:], AF.Sqrt)
                s.drain()
                s.nop().then_inc(sC.h, 1)
            sC.n += 1
            std_done = sC.n

            @block.vector
            def _(v):
                v.wait_ge(sC.h, std_done)
                v.reciprocal(sm8[:], sm7[:])
                v.drain()
                v.scalar_tensor_tensor(sm7[:], sm6[:], -1.0, sm8[:], ALU.mult, ALU.mult)
                v.drain()
                v.nop().then_inc(sV.h, 1)
            sV.n += 1
            r_done = sV.n

            @block.scalar
            def _(s):
                s.wait_ge(sV.h, r_done)
                s.activation(big[:, dst_off:dst_off + D], xs3(j), AF.Identity,
                             bias=sm7[:], scale=sm8[:])
                s.drain()
                s.nop().then_inc(sC.h, 1)
            sC.n += 1
            return sC.n

        def emit_ln_stats(n, pre_sync=None, scr0=None, scr1=None):
            scr_ap0 = big[:, BIG_SQ:BIG_SQ + D] if scr0 is None else scr0
            scr_ap1 = big[:, BIG_ST:BIG_ST + D] if scr1 is None else scr1
            @block.vector
            def _(v):
                if pre_sync is not None:
                    v.wait_ge(sS.h, pre_sync)
                for j in range(n):
                    scr = scr_ap0 if j % 2 == 0 else scr_ap1
                    v.tensor_tensor(scr, xs3(j), xs3(j), ALU.mult)
                    v.drain()
                    v.tensor_reduce(L_sb[:, j:j + 1], scr, AX.X, ALU.add)
                    v.tensor_reduce(M_sb[:, j:j + 1], xs3(j), AX.X, ALU.add)
                    v.drain()
                v.tensor_scalar(M_sb[:, 0:n], M_sb[:, 0:n], 1.0 / D, None, ALU.mult)
                v.tensor_scalar(L_sb[:, 0:n], L_sb[:, 0:n], 1.0 / D, None, ALU.mult)
                v.drain()
                v.tensor_tensor(cnt_sb[:, 0:n], M_sb[:, 0:n], M_sb[:, 0:n], ALU.mult)
                v.drain()
                v.tensor_tensor(L_sb[:, 0:n], L_sb[:, 0:n], cnt_sb[:, 0:n], ALU.subtract)
                v.drain()
                v.tensor_scalar(L_sb[:, 0:n], L_sb[:, 0:n], EPS, None, ALU.add)
                v.drain()
                v.nop().then_inc(sV.h, 1)
            sV.n += 1
            var_done = sV.n

            @block.scalar
            def _(s):
                s.wait_ge(sV.h, var_done)
                s.activation(cnt_sb[:, 0:n], L_sb[:, 0:n], AF.Sqrt)
                s.drain()
                s.nop().then_inc(sC.h, 1)
            sC.n += 1
            std_done = sC.n

            @block.vector
            def _(v):
                v.wait_ge(sC.h, std_done)
                v.reciprocal(nM_sb[:, 0:n], cnt_sb[:, 0:n])
                v.drain()
                v.scalar_tensor_tensor(cnt_sb[:, 0:n], M_sb[:, 0:n], -1.0,
                                       nM_sb[:, 0:n], ALU.mult, ALU.mult)
                v.drain()
                v.nop().then_inc(sV.h, 1)
            sV.n += 1
            return sV.n

        def emit_ffn(tag, cut=None):
            @block.gpsimd
            def _(g):
                g.wait_ge(sV.h, sV.n)
                g.wait_ge(sC.h, sC.n)
                g.memset(qf[:, 0:8192], 0.0)
                g.nop().then_inc(sP.h, 1)
                g.wait_ge(sP.h, sP.n + 1)
                for a in range(8):
                    g.dma_start(
                        out=pout[a * 1024:(a + 1) * 1024, :].rearrange("(b p) d -> p b d", p=128),
                        in_=qf[:, 0:8192].rearrange("p (b d) -> p b d", d=D),
                    ).then_inc(sG.h, 16)
            sP.n += 1
            sG.n += 8 * 16
            zero_done = sG.n

            @block.sync
            def _(s):
                s.dma_start(out=big[:, BIG_B2:BIG_B2 + D],
                            in_=b2_in.rearrange("(p d) -> p d", p=1)
                            .to_broadcast((128, D))).then_inc(sS.h, 16)
            sS.n += 16
            b2_done = sS.n

            if cut == "zero":
                return True
            if tag == "f2":
                @block.gpsimd
                def _(g):
                    g.wait_ge(sV.h, sV.n)
                    g.dma_start(out=idx16p[0:16, :],
                                in_=pspill[0:1, :].bitcast(I32).rearrange("q (s p) -> (q p) s", p=16),
                                allow_slow_non_contiguous=True).then_inc(sG.h, 16)
                    g.dma_start(out=ixu[:],
                                in_=pspill[0:1, :].bitcast(I32).rearrange("q (b p) -> (q p) b", p=128),
                                allow_slow_non_contiguous=True).then_inc(sG.h, 16)
                    g.wait_ge(sG.h, sG.n + 32)
                    for gi in range(1, 8):
                        g.dma_start(out=idx16p[16 * gi:16 * (gi + 1), :],
                                    in_=idx16p[0:16, :]).then_inc(sG.h, 16)
                sG.n += 9 * 16
                zero_done = sG.n

            if tag == "f1":
                ln_stats_done = emit_ln_stats(NLO, pre_sync=P0_SYNC)
                ydma = {}
                for j in range(NLO):
                    zofs = BIG_Z if j % 2 == 0 else BIG_ST

                    @block.scalar
                    def _(s, j=j, zofs=zofs, ln_stats_done=ln_stats_done):
                        if j == 0:
                            s.wait_ge(sV.h, ln_stats_done)
                        if j >= 2:
                            s.wait_ge(sS.h, ydma[j - 2])
                        s.activation(big[:, zofs:zofs + D], xs3(j), AF.Identity,
                                     bias=cnt_sb[:, j:j + 1], scale=nM_sb[:, j:j + 1])
                        s.drain()
                        s.nop().then_inc(sC.h, 1)
                    sC.n += 1
                    y_done = sC.n

                    @block.sync
                    def _(s, j=j, y_done=y_done, zofs=zofs):
                        s.wait_ge(sC.h, y_done)
                        s.dma_start(out=y_agin[j * 128:(j + 1) * 128, :],
                                    in_=big[:, zofs:zofs + D]).then_inc(sS.h, 16)
                    sS.n += 16
                    ydma[j] = sS.n
            else:
                ln_stats_done = emit_ln_stats(NLO,
                                              scr0=qf[:, 8192:8192 + D],
                                              scr1=qf[:, 9216:9216 + D])
                ydma = {}
                for j in range(NLO):
                    zofs = 10240 if j % 2 == 0 else 11264

                    @block.scalar
                    def _(s, j=j, zofs=zofs, ln_stats_done=ln_stats_done):
                        if j == 0:
                            s.wait_ge(sV.h, ln_stats_done)
                        if j >= 2:
                            s.wait_ge(sS.h, ydma[j - 2])
                        s.activation(qf[:, zofs:zofs + D], xs3(j), AF.Identity,
                                     bias=cnt_sb[:, j:j + 1], scale=nM_sb[:, j:j + 1])
                        s.drain()
                        s.nop().then_inc(sC.h, 1)
                    sC.n += 1
                    y_done = sC.n

                    @block.sync
                    def _(s, j=j, y_done=y_done, zofs=zofs):
                        s.wait_ge(sC.h, y_done)
                        s.dma_start(out=y_agin[j * 128:(j + 1) * 128, :],
                                    in_=qf[:, zofs:zofs + D]).then_inc(sS.h, 16)
                    sS.n += 16
                    ydma[j] = sS.n

            if cut == "ln":
                return True
            @block.gpsimd
            def _(g):
                g.wait_ge(sS.h, sS.n)
                g.collective_compute("AllGather", ALU.bypass,
                                     replica_groups=[list(range(NCORES))],
                                     ins=[y_agin], outs=[y_all]).then_inc(sCC.h, 1)
            sCC.n += 1
            ag_done = sCC.n
            if cut == "ag":
                return True

            if "dbg_yall" in dbg_aps and tag == "f1":
                @block.gpsimd
                def _(g):
                    g.wait_ge(sCC.h, ag_done)
                    for a in range(8):
                        g.dma_start(
                            out=dbg_aps["dbg_yall"][a * 1024:(a + 1) * 1024, :]
                                .rearrange("(b p) d -> p b d", p=128),
                            in_=y_all[a * 1024:(a + 1) * 1024, :]
                                .rearrange("(b p) d -> p b d", p=128),
                        ).then_inc(sG.h, 16)
                sG.n += 8 * 16

            gelu_dones = []
            mm1_dones = []
            mm2_dones = []
            out_evac_done = sC.n
            tr_evac = {}
            blk_tr_last = {}
            blk_add_last = {-1: sV.n}
            blk_scat_last = {-1: sG.n}
            g_dones = {}

            def emit_gather(b):
                @block.gpsimd
                def _(g, b=b):
                    g.wait_ge(sCC.h, ag_done)
                    if b == 0:
                        g.wait_ge(sG.h, zero_done)
                        g.wait_ge(sV.h, P0_VEC2)
                    elif b >= 2:
                        g.wait_ge(sT.h, blk_tr_last[b - 2])
                    g.dma_gather(
                        out_ap=qf[:, QF_G + 2048 * (b % 2):
                                     QF_G + 2048 * (b % 2) + 2 * D]
                            .rearrange("p (b d) -> p b d", d=D),
                        in_ap=y_all, idxs_ap=idx16p[:, 16 * b:16 * (b + 1)],
                        num_idxs=SB, num_idxs_reg=SB, elem_size=D,
                        single_packet=False).then_inc(sG.h, 16)
                sG.n += 16
                g_dones[b] = sG.n

            emit_gather(0)
            for b in range(NSB):
                gofs = 2048 * (b % 2)

                for grp in range(4):
                    bank_i = 4 + (grp % 2)

                    @block.tensor
                    def _(t, b=b, grp=grp, bank_i=bank_i, gofs=gofs):
                        t.wait_ge(sG.h, g_dones[b])
                        if b == 0 and grp == 0:
                            t.wait_ge(sV.h, P0_VEC)
                            t.wait_ge(sV.h, blk_add_last[-1])
                        if bank_i in tr_evac:
                            t.wait_ge(sV.h, tr_evac[bank_i])
                        if grp < 2 and len(gelu_dones) >= 4 - grp:
                            t.wait_ge(sC.h, gelu_dones[-(4 - grp)])
                        for u in range(4):
                            a4 = grp * 4 + u
                            a_, s_ = a4 >> 1, a4 & 1
                            instr = t.transpose(
                                pb[bank_i][:, u * 128:(u + 1) * 128],
                                qf[:, QF_G + gofs + s_ * D + a_ * 128:
                                       QF_G + gofs + s_ * D + (a_ + 1) * 128],
                                ident[:])
                        instr.then_inc(sT.h, 1)
                    sT.n += 1
                    tr_done = sT.n

                    @block.vector
                    def _(v, grp=grp, bank_i=bank_i, tr_done=tr_done, gofs=gofs):
                        v.wait_ge(sT.h, tr_done)
                        a0 = (grp * 4) >> 1
                        v.tensor_copy(qf[:, QF_GT + gofs + a0 * SB:
                                             QF_GT + gofs + a0 * SB + 512],
                                      pb[bank_i][:])
                        v.drain()
                        v.nop().then_inc(sV.h, 1)
                    sV.n += 1
                    tr_evac[bank_i] = sV.n
                gt_done = sV.n
                blk_tr_last[b] = sT.n
                if b + 1 < NSB:
                    emit_gather(b + 1)

                for f in range(FFCH):
                    @block.sync
                    def _(s, f=f, b=b, nmm2=len(mm2_dones)):
                        if nmm2 >= 2:
                            s.wait_ge(sT.h, mm2_dones[nmm2 - 2])
                        s.dma_start(
                            out=big[:, BIG_W1 + (f % 2) * 1024: BIG_W1 + (f % 2) * 1024 + 1024]
                                .rearrange("p (a w) -> p a w", w=128),
                            in_=w1_in[:, f * 128:(f + 1) * 128].rearrange("(a p) w -> p a w", p=128),
                        ).then_inc(sS.h, 16)
                        s.dma_start(
                            out=big[:, 10240 + (f % 2) * 1024: 10240 + (f % 2) * 1024 + 1024],
                            in_=w2_in[f * 128:(f + 1) * 128, :],
                        ).then_inc(sS.h, 16)
                    sS.n += 32
                    w_done = sS.n

                    @block.tensor
                    def _(t, b=b, f=f, w_done=w_done, gt_done=gt_done, gofs=gofs,
                          ngelu=len(gelu_dones)):
                        t.wait_ge(sS.h, w_done)
                        if f == 0:
                            t.wait_ge(sV.h, gt_done)
                            if b == 0:
                                t.wait_ge(sC.h, out_evac_done)
                        if ngelu >= 4:
                            t.wait_ge(sC.h, gelu_dones[ngelu - 4])
                        hb = pb[4 + (f % 4)]
                        for a in range(DCH):
                            instr = t.matmul(
                                hb[:, 0:SB],
                                big[:, BIG_W1 + (f % 2) * 1024 + a * 128:
                                       BIG_W1 + (f % 2) * 1024 + (a + 1) * 128],
                                qf[:, QF_GT + gofs + a * SB: QF_GT + gofs + (a + 1) * SB],
                                start=(a == 0), stop=(a == DCH - 1))
                        instr.then_inc(sT.h, 1)
                    sT.n += 1
                    mm1_dones.append(sT.n)

                    @block.scalar
                    def _(s, f=f, mm1_done=sT.n):
                        s.wait_ge(sT.h, mm1_done)
                        s.activation(big[:, BIG_H + f * SB: BIG_H + (f + 1) * SB],
                                     pb[4 + (f % 4)][:, 0:SB], AF.Gelu_apprx_tanh,
                                     bias=b1s[:, f:f + 1])
                        s.drain()
                        s.nop().then_inc(sC.h, 1)
                    sC.n += 1
                    gelu_dones.append(sC.n)

                    @block.tensor
                    def _(t, b=b, f=f, gdone=gelu_dones[-1]):
                        t.wait_ge(sC.h, gdone)
                        if f == 0:
                            t.wait_ge(sV.h, blk_add_last[b - 1])
                        for s_ in range(2):
                            for h_ in range(2):
                                instr = t.matmul(
                                    pb[s_ * 2 + h_][:],
                                    big[:, BIG_H + f * SB + s_ * 128:
                                           BIG_H + f * SB + (s_ + 1) * 128],
                                    big[:, 10240 + (f % 2) * 1024 + h_ * 512:
                                            10240 + (f % 2) * 1024 + (h_ + 1) * 512],
                                    start=(f == 0), stop=(f == FFCH - 1))
                        instr.then_inc(sT.h, 1)
                    sT.n += 1
                    mm2_dones.append(sT.n)
                mm2_done = sT.n

                for s_ in range(2):
                    @block.vector
                    def _(v, b=b, s_=s_, mm2_done=mm2_done, scn=sC.n):
                        v.wait_ge(sT.h, mm2_done)
                        if b == 0 and s_ == 0:
                            v.wait_ge(sS.h, b2_done)
                        if s_ == 0:
                            v.wait_ge(sC.h, scn)
                        else:
                            v.wait_ge(sC.h, sC.n)
                        for h_ in range(2):
                            v.tensor_tensor(
                                qf[:, QF_CMP + h_ * 512: QF_CMP + (h_ + 1) * 512],
                                pb[s_ * 2 + h_][:],
                                big[:, BIG_B2 + h_ * 512: BIG_B2 + (h_ + 1) * 512],
                                ALU.add)
                        v.drain()
                        v.nop().then_inc(sV.h, 1)
                    sV.n += 1
                    add_done = sV.n

                    @block.scalar
                    def _(s, b=b, s_=s_, add_done=add_done):
                        s.wait_ge(sV.h, add_done)
                        s.wait_ge(sG.h, blk_scat_last[b - 1])
                        for h_ in range(2):
                            s.activation(
                                qf[:, QF_SC + s_ * D + h_ * 512: QF_SC + s_ * D + (h_ + 1) * 512],
                                qf[:, QF_CMP + h_ * 512: QF_CMP + (h_ + 1) * 512],
                                AF.Identity,
                                scale=scl_sb[:, 2 * b + s_: 2 * b + s_ + 1])
                        s.drain()
                        s.nop().then_inc(sC.h, 1)
                    sC.n += 1
                    ev_done = sC.n

                    @block.gpsimd
                    def _(g, b=b, s_=s_, ev_done=ev_done):
                        g.wait_ge(sC.h, ev_done)
                        if "noscat" in ablate:
                            g.dma_start(
                                out=pout[(2 * b + s_) * 128:(2 * b + s_ + 1) * 128, :],
                                in_=qf[:, QF_SC + s_ * D: QF_SC + (s_ + 1) * D]).then_inc(sG.h, 16)
                        else:
                            g.indirect_dma_start(
                                out=pout,
                                out_offset=IndirectOffsetOnAxis(
                                    ixu[:, 2 * b + s_: 2 * b + s_ + 1], 0),
                                in_=qf[:, QF_SC + s_ * D: QF_SC + (s_ + 1) * D],
                                in_offset=None).then_inc(sG.h, 16)
                    sG.n += 16
                blk_add_last[b] = sV.n
                blk_scat_last[b] = sG.n

            if "dbg_h" in dbg_aps and tag == "f1":
                @block.sync
                def _(s):
                    s.wait_ge(sC.h, sC.n)
                    s.wait_ge(sV.h, sV.n)
                    s.dma_start(out=dbg_aps["dbg_h"][:, 0:8192], in_=big[:, 0:8192]).then_inc(sS.h, 16)
                    s.dma_start(out=dbg_aps["dbg_gt"][:, 0:8192], in_=qf[:, 0:8192]).then_inc(sS.h, 16)
                sS.n += 32

            @block.gpsimd
            def _(g):
                g.wait_ge(sG.h, sG.n)
                g.collective_compute("ReduceScatter", ALU.add,
                                     replica_groups=[list(range(NCORES))],
                                     ins=[pout], outs=[rs_out]).then_inc(sCC.h, 1)
            sCC.n += 1
            rs_done = sCC.n
            if cut == "rs":
                return True

            if "dbg_rs" in dbg_aps and tag == "f1":
                @block.gpsimd
                def _(g):
                    g.wait_ge(sCC.h, rs_done)
                    g.dma_start(out=dbg_aps["dbg_rs"].rearrange("(b p) d -> p b d", p=128),
                                in_=rs_out.rearrange("(b p) d -> p b d", p=128)).then_inc(sG.h, 16)
                sG.n += 16
            if "dbg_pout" in dbg_aps and tag == "f1":
                @block.gpsimd
                def _(g):
                    g.wait_ge(sCC.h, rs_done)
                    for a in range(8):
                        g.dma_start(
                            out=dbg_aps["dbg_pout"][a * 1024:(a + 1) * 1024, :]
                                .rearrange("(b p) d -> p b d", p=128),
                            in_=pout[a * 1024:(a + 1) * 1024, :]
                                .rearrange("(b p) d -> p b d", p=128)).then_inc(sG.h, 16)
                sG.n += 8 * 16

            for j in range(NLO):
                @block.sync
                def _(s, j=j, rs_done=rs_done):
                    if j == 0:
                        s.wait_ge(sCC.h, rs_done)
                    else:
                        s.wait_ge(sV.h, sV.n)
                    s.dma_start(out=big[:, BIG_Z:BIG_Z + D],
                                in_=rs_out[j * 128:(j + 1) * 128, :]).then_inc(sS.h, 16)
                sS.n += 16
                ld_done = sS.n

                @block.vector
                def _(v, j=j, ld_done=ld_done):
                    v.wait_ge(sS.h, ld_done)
                    v.scalar_tensor_tensor(xs3(j), big[:, BIG_Z:BIG_Z + D], 0.5,
                                           xs3(j), ALU.mult, ALU.add)
                    v.drain()
                    v.nop().then_inc(sV.h, 1)
                sV.n += 1

        def emit_tail():
            @block.sync
            def _(s):
                s.wait_ge(sV.h, sV.n)
                s.wait_ge(sC.h, sC.n)
                s.wait_ge(sT.h, sT.n)
                s.wait_ge(sP.h, sP.n)
                s.wait_ge(sCC.h, sCC.n)
                s.wait_ge(sG.h, sG.n)
                s.dma_start(out=x_out.rearrange("(j p) d -> p j d", p=128),
                            in_=xs[:].rearrange("p (j d) -> p j d", d=D)).then_inc(sS.h, 16)
                sS.n += 16
                s.wait_ge(sS.h, sS.n)

        if emit_ffn("f1", cut=cut):
            emit_tail()
            return nc
        if nphases < 2:
            emit_tail()
            return nc

        p2_stats_done = emit_ln_stats(NT)
        p2_trl = {}
        p2_ev = {}
        for j in range(NT):
            zofs = BIG_Z if j % 2 == 0 else BIG_ST

            @block.scalar
            def _(s, j=j, zofs=zofs):
                if j == 0:
                    s.wait_ge(sV.h, p2_stats_done)
                if j >= 2:
                    s.wait_ge(sT.h, p2_trl[j - 2])
                s.activation(big[:, zofs:zofs + D], xs3(j), AF.Identity,
                             bias=cnt_sb[:, j:j + 1], scale=nM_sb[:, j:j + 1])
                s.drain()
                s.nop().then_inc(sC.h, 1)
            sC.n += 1
            z_done = sC.n

            @block.sync
            def _(s, j=j, z_done=z_done):
                s.wait_ge(sC.h, z_done)
                s.dma_start(out=xspill[j * 128:(j + 1) * 128, :], in_=xs3(j)).then_inc(sS.h, 16)
            sS.n += 16
            spill_done = sS.n

            for half in range(2):
                @block.tensor
                def _(t, j=j, half=half, z_done=z_done, spill_done=spill_done, zofs=zofs):
                    t.wait_ge(sC.h, z_done)
                    if half == 0:
                        t.wait_ge(sS.h, spill_done)
                    if j >= 1:
                        t.wait_ge(sV.h, p2_ev[(j - 1, half)])
                    bank = pb[4 + half]
                    for u in range(4):
                        a_ = half * 4 + u
                        instr = t.transpose(bank[:, u * 128:(u + 1) * 128],
                                            big[:, zofs + a_ * 128: zofs + (a_ + 1) * 128],
                                            ident[:])
                    instr.then_inc(sT.h, 1)
                sT.n += 1
                tr_done = sT.n

                @block.vector
                def _(v, j=j, half=half, tr_done=tr_done):
                    v.wait_ge(sT.h, tr_done)
                    v.tensor_copy(xs[:, j * D + half * 512: j * D + (half + 1) * 512],
                                  pb[4 + half][:])
                    v.drain()
                    v.nop().then_inc(sV.h, 1)
                sV.n += 1
                p2_ev[(j, half)] = sV.n
            p2_trl[j] = sT.n
        zt_done_v = sV.n

        prev_kdma = 0
        prev_vtdma = 0
        prev_vt_done = 0
        for o in range(DCH):
            @block.sync
            def _(s, o=o):
                if o >= 1:
                    s.wait_ge(sT.h, sT.n)
                for wi, wap in enumerate((wq_in, wk_in, wv_in)):
                    s.dma_start(
                        out=big[:, BIG_W1 + wi * 1024: BIG_W1 + (wi + 1) * 1024]
                            .rearrange("p (a w) -> p a w", w=128),
                        in_=wap[:, o * 128:(o + 1) * 128].rearrange("(a p) w -> p a w", p=128),
                    ).then_inc(sS.h, 16)
            sS.n += 48
            w_done = sS.n

            for tb in range(4):
                @block.tensor
                def _(t, o=o, tb=tb, w_done=w_done):
                    t.wait_ge(sS.h, w_done)
                    if o == 0 and tb == 0:
                        t.wait_ge(sV.h, zt_done_v)
                        t.wait_ge(sC.h, sC.n)
                    else:
                        t.wait_ge(sC.h, sC.n)
                    for wi in range(3):
                        for a in range(DCH):
                            instr = t.matmul(
                                pb[wi][:],
                                big[:, BIG_W1 + wi * 1024 + a * 128:
                                       BIG_W1 + wi * 1024 + (a + 1) * 128],
                                xs[:].rearrange("p (j d) -> p j d", d=D)
                                    [:, tb * 4:(tb + 1) * 4, a * 128:(a + 1) * 128],
                                start=(a == 0), stop=(a == DCH - 1))
                    instr.then_inc(sT.h, 1)
                sT.n += 1
                qkv_mm = sT.n

                @block.scalar
                def _(s, o=o, tb=tb, qkv_mm=qkv_mm, prev_kdma=prev_kdma,
                      prev_vt_done=prev_vt_done):
                    s.wait_ge(sT.h, qkv_mm)
                    s.wait_ge(sS.h, prev_kdma)
                    s.wait_ge(sT.h, prev_vt_done)
                    s.activation(qf[:, o * 2048 + tb * 512: o * 2048 + (tb + 1) * 512],
                                 pb[0][:], AF.Identity, bias=bqkv[:, o:o + 1])
                    s.activation(big[:, BIG_ST:BIG_ST + 512], pb[1][:], AF.Identity,
                                 bias=bqkv[:, 8 + o:8 + o + 1])
                    s.activation(big[:, BIG_ST + 512:BIG_ST + 1024], pb[2][:], AF.Identity,
                                 bias=bqkv[:, 16 + o:16 + o + 1])
                    s.drain()
                    s.nop().then_inc(sC.h, 1)
                sC.n += 1
                ev_done = sC.n

                @block.sync
                def _(s, o=o, tb=tb, ev_done=ev_done):
                    s.wait_ge(sC.h, ev_done)
                    kvag = kv_agin_lo if tb < 2 else kv_agin_hi
                    tb_ = tb % 2
                    s.dma_start(out=kvag[o * 128:(o + 1) * 128, tb_ * 512:(tb_ + 1) * 512],
                                in_=big[:, BIG_ST:BIG_ST + 512]).then_inc(sS.h, 16)
                sS.n += 16
                prev_kdma = sS.n

                @block.tensor
                def _(t, ev_done=ev_done, prev_vtdma=prev_vtdma):
                    t.wait_ge(sC.h, ev_done)
                    t.wait_ge(sV.h, sV.n)
                    for u in range(4):
                        instr = t.transpose(pb[3][:, u * 128:(u + 1) * 128],
                                            big[:, BIG_ST + 512 + u * 128:
                                                   BIG_ST + 512 + (u + 1) * 128],
                                            ident[:])
                    instr.then_inc(sT.h, 1)
                sT.n += 1
                prev_vt_done = sT.n

                @block.vector
                def _(v, prev_vt_done=prev_vt_done, prev_vtdma=prev_vtdma):
                    v.wait_ge(sT.h, prev_vt_done)
                    v.wait_ge(sS.h, prev_vtdma)
                    v.tensor_copy(pt_sb[:], pb[3][:])
                    v.drain()
                    v.nop().then_inc(sV.h, 1)
                sV.n += 1
                vtev = sV.n

                @block.sync
                def _(s, o=o, tb=tb, vtev=vtev):
                    s.wait_ge(sV.h, vtev)
                    kvag = kv_agin_lo if tb < 2 else kv_agin_hi
                    tb_ = tb % 2
                    s.dma_start(
                        out=kvag[1024 + tb_ * 512:1024 + (tb_ + 1) * 512,
                                 o * 128:(o + 1) * 128]
                            .rearrange("(u p) w -> p u w", p=128),
                        in_=pt_sb[:].rearrange("p (u w) -> p u w", w=128),
                    ).then_inc(sS.h, 16)
                sS.n += 16
                prev_vtdma = sS.n

        @block.gpsimd
        def _(g):
            g.wait_ge(sS.h, sS.n)
            g.collective_compute("AllGather", ALU.bypass,
                                 replica_groups=[list(range(NCORES))],
                                 ins=[kv_agin_lo], outs=[kv_all_lo]).then_inc(sCC.h, 1)
            g.collective_compute("AllGather", ALU.bypass,
                                 replica_groups=[list(range(NCORES))],
                                 ins=[kv_agin_hi], outs=[kv_all_hi]).then_inc(sCC.h, 1)
        sCC.n += 2
        kv_ag_lo = sCC.n - 1
        kv_ag = sCC.n

        if "dbg_q" in dbg_aps:
            @block.sync
            def _(s):
                s.wait_ge(sC.h, sC.n)
                s.dma_start(out=dbg_aps["dbg_q"], in_=qf[:]).then_inc(sS.h, 16)
                sS.n += 16

        if nphases < 3:
            emit_tail()
            return nc

        exp_done_at = {}
        mask_done_at = {}
        ptd_at = {}
        ptev_at = {}
        av_at = {}
        oupd_at = {}
        kvld_at = {}
        kb_last_vi = {}
        kb_of_vi = {}

        def emit_kv_load(kb):
            @block.sync
            def _(s, kb=kb):
                if kb <= 1:
                    s.wait_ge(sCC.h, kv_ag_lo)
                elif kb >= 2:
                    s.wait_ge(sT.h, av_at[kb_last_vi[kb - 2]])
                if kb >= 16 and kb <= 17:
                    s.wait_ge(sCC.h, kv_ag)
                base = 8192 * (kb % 2)
                for i in range(4):
                    gtile = 4 * kb + i
                    r, lj = gtile & 7, gtile >> 3
                    kva = kv_all_lo if lj < 8 else kv_all_hi
                    lj_ = lj % 8
                    s.dma_start(
                        out=xs[:, base:base + 4096].rearrange("p (a w) -> p a w", w=512)
                            [:, :, i * 128:(i + 1) * 128],
                        in_=kva[2048 * r:2048 * r + 1024, 128 * lj_:128 * (lj_ + 1)]
                            .rearrange("(a p) w -> p a w", p=128),
                    ).then_inc(sS.h, 16)
                    s.dma_start(
                        out=xs[:, base + 4096 + i * 1024: base + 4096 + (i + 1) * 1024],
                        in_=kva[2048 * r + 1024 + 128 * lj_:
                                2048 * r + 1024 + 128 * (lj_ + 1), :],
                    ).then_inc(sS.h, 16)
            sS.n += 8 * 16
            kvld_at[kb] = sS.n

        def emit_scores(vi_, j, kb):
            sbk = vi_ % 3
            base = 8192 * (kb % 2)

            @block.tensor
            def _(t, vi_=vi_, j=j, kb=kb, sbk=sbk, base=base):
                if vi_ == 0 or kb_of_vi.get(vi_ - 1) != kb:
                    t.wait_ge(sS.h, kvld_at[kb])
                if vi_ - 3 in exp_done_at:
                    t.wait_ge(sC.h, exp_done_at[vi_ - 3])
                for a in range(DCH):
                    instr = t.matmul(
                        pb[sbk][:],
                        qf[:, a * 2048 + j * 128: a * 2048 + (j + 1) * 128],
                        xs[:, base:base + 4096].rearrange("p (a w) -> p a w", w=512)[:, a, :],
                        start=(a == 0), stop=(a == DCH - 1))
                instr.then_inc(sT.h, 1)
            sT.n += 1
            return sT.n

        def emit_rest(vi_, j, kb, sc_at):
            first = (kb == 0)
            sbk = vi_ % 3
            tbk = 3
            ab0 = 4 + 2 * (vi_ % 2)
            ab1 = ab0 + 1
            pofs = 512 * (vi_ % 3)

            if first:
                @block.vector
                def _(v, j=j, sbk=sbk, sc_at=sc_at):
                    v.wait_ge(sT.h, sc_at)
                    v.tensor_reduce(sm1[:], pb[sbk][:], AX.X, ALU.max)
                    v.drain()
                    v.tensor_scalar(sm1[:], sm1[:], INV_SQ_D, None, ALU.mult)
                    v.drain()
                    v.tensor_copy(M_sb[:, j:j + 1], sm1[:])
                    v.tensor_scalar(nM_sb[:, j:j + 1], sm1[:], -1.0, None, ALU.mult)
                    v.drain()
                    v.nop().then_inc(sV.h, 1)
                sV.n += 1
            stats_at = sV.n

            @block.scalar
            def _(s, vi_=vi_, j=j, sbk=sbk, pofs=pofs, sc_at=sc_at,
                  first=first, stats_at=stats_at):
                s.wait_ge(sT.h, sc_at)
                if first:
                    s.wait_ge(sV.h, stats_at)
                if vi_ - 3 in ptd_at:
                    s.wait_ge(sT.h, ptd_at[vi_ - 3])
                s.activation(pm_sb[:, pofs:pofs + 512], pb[sbk][:], AF.Exp,
                             scale=INV_SQ_D, bias=nM_sb[:, j:j + 1])
                s.drain()
                s.nop().then_inc(sC.h, 1)
            sC.n += 1
            exp_done_at[vi_] = sC.n

            @block.vector
            def _(v, vi_=vi_, j=j, first=first, pofs=pofs):
                v.wait_ge(sC.h, exp_done_at[vi_])
                v.scalar_tensor_tensor(pm_sb[:, pofs:pofs + 512], iota_f[:],
                                       lim_sb[:, vi_:vi_ + 1],
                                       pm_sb[:, pofs:pofs + 512], ALU.is_le, ALU.mult)
                v.drain()
                v.tensor_reduce(sm5[:], pm_sb[:, pofs:pofs + 512], AX.X, ALU.add)
                v.drain()
                if first:
                    v.tensor_copy(L_sb[:, j:j + 1], sm5[:])
                else:
                    v.tensor_tensor(L_sb[:, j:j + 1], L_sb[:, j:j + 1], sm5[:], ALU.add)
                v.drain()
                v.nop().then_inc(sV.h, 1)
            sV.n += 1
            mask_done_at[vi_] = sV.n

            @block.tensor
            def _(t, vi_=vi_, tbk=tbk, pofs=pofs):
                t.wait_ge(sV.h, mask_done_at[vi_])
                if vi_ - 1 in ptev_at:
                    t.wait_ge(sV.h, ptev_at[vi_ - 1])
                for u in range(4):
                    instr = t.transpose(pb[tbk][:, u * 128:(u + 1) * 128],
                                        pm_sb[:, pofs + u * 128: pofs + (u + 1) * 128],
                                        ident[:])
                instr.then_inc(sT.h, 1)
            sT.n += 1
            ptd_at[vi_] = sT.n

            @block.vector
            def _(v, vi_=vi_, tbk=tbk):
                v.wait_ge(sT.h, ptd_at[vi_])
                if vi_ - 1 in av_at:
                    v.wait_ge(sT.h, av_at[vi_ - 1])
                v.tensor_copy(pt_sb[:], pb[tbk][:])
                v.drain()
                v.nop().then_inc(sV.h, 1)
            sV.n += 1
            ptev_at[vi_] = sV.n

            @block.tensor
            def _(t, vi_=vi_, kb=kb, ab0=ab0, ab1=ab1, pofs=pofs):
                t.wait_ge(sV.h, ptev_at[vi_])
                if vi_ - 2 in oupd_at:
                    t.wait_ge(sV.h, oupd_at[vi_ - 2])
                base = 8192 * (kb % 2)
                for h_ in range(2):
                    for u in range(4):
                        instr = t.matmul(
                            pb[ab0 + h_][:],
                            pt_sb[:, u * 128:(u + 1) * 128],
                            xs[:, base + 4096 + u * 1024 + h_ * 512:
                                   base + 4096 + u * 1024 + (h_ + 1) * 512],
                            start=(u == 0), stop=(u == 3))
                instr.then_inc(sT.h, 1)
            sT.n += 1
            av_at[vi_] = sT.n

            @block.vector
            def _(v, vi_=vi_, j=j, first=first, ab0=ab0, ab1=ab1):
                v.wait_ge(sT.h, av_at[vi_])
                Oj0 = big[:, j * 1024: j * 1024 + 512]
                Oj1 = big[:, j * 1024 + 512: (j + 1) * 1024]
                if first:
                    v.tensor_copy(Oj0, pb[ab0][:])
                    v.tensor_copy(Oj1, pb[ab1][:])
                else:
                    v.tensor_tensor(Oj0, Oj0, pb[ab0][:], ALU.add)
                    v.tensor_tensor(Oj1, Oj1, pb[ab1][:], ALU.add)
                v.drain()
                v.nop().then_inc(sV.h, 1)
            sV.n += 1
            oupd_at[vi_] = sV.n

        visits = []
        for kb in range(NKB):
            for j in range(kb // 2, NT):
                visits.append((kb, j))
                kb_of_vi[len(visits) - 1] = kb
                kb_last_vi[kb] = len(visits) - 1

        sc_ats = {}
        emit_kv_load(0)
        emit_kv_load(1)
        NV_ = len(visits)
        rest_next = [0]

        def emit_rest_one():
            rn = rest_next[0]
            pkb, pj = visits[rn]
            emit_rest(rn, pj, pkb, sc_ats[rn])
            if rn == kb_last_vi[pkb] and pkb + 2 < NKB:
                emit_kv_load(pkb + 2)
            rest_next[0] = rn + 1

        for vi_, (kb, j) in enumerate(visits):
            while kb not in kvld_at:
                emit_rest_one()
            sc_ats[vi_] = emit_scores(vi_, j, kb)
            while rest_next[0] <= vi_ - 2:
                emit_rest_one()
        while rest_next[0] < NV_:
            emit_rest_one()


        if "dbg_ml" in dbg_aps:
            @block.sync
            def _(s):
                s.wait_ge(sV.h, sV.n)
                s.dma_start(out=dbg_aps["dbg_ml"][:, 0:16], in_=M_sb[:]).then_inc(sS.h, 16)
                s.dma_start(out=dbg_aps["dbg_ml"][:, 16:32], in_=L_sb[:]).then_inc(sS.h, 16)
                sS.n += 32

        if nphases < 4:
            emit_tail()
            return nc

        @block.vector
        def _(v):
            v.wait_ge(sV.h, sV.n)
            v.memset(ones1[:], 1.0)
            v.tensor_reduce(sm1[:], M_sb[:], AX.X, ALU.max)
            v.drain()
            v.nop().then_inc(sV.h, 1)
        sV.n += 1
        mrow_done = sV.n

        @block.tensor
        def _(t, mrow_done=mrow_done):
            t.wait_ge(sV.h, mrow_done)
            t.transpose(pb[0][0:1, 0:128], sm1[:], ident[:]).then_inc(sT.h, 1)
        sT.n += 1
        mt_done = sT.n

        @block.vector
        def _(v, mt_done=mt_done):
            v.wait_ge(sT.h, mt_done)
            v.tensor_copy(pm_sb[0:1, 1024:1152], pb[0][0:1, 0:128])
            v.drain()
            v.tensor_reduce(mzs[:, :], pm_sb[0:1, 1024:1152], AX.X, ALU.max)
            v.drain()
            v.tensor_scalar(mzs[:, :], mzs[:, :], -1.0, None, ALU.mult)
            v.drain()
            v.nop().then_inc(sV.h, 1)
        sV.n += 1
        mloc_done = sV.n

        @block.sync
        def _(s, mloc_done=mloc_done):
            s.wait_ge(sV.h, mloc_done)
            s.dma_start(out=bscr[:, 0:1], in_=mzs[:, 0:1]).then_inc(sS.h, 16)
            s.wait_ge(sS.h, sS.n + 16)
            s.dma_start(out=sm3[:], in_=bscr[:, 0:1].to_broadcast((128, 1))).then_inc(sS.h, 16)
        sS.n += 32
        nm_bcast = sS.n

        @block.scalar
        def _(s, nm_bcast=nm_bcast):
            s.wait_ge(sS.h, nm_bcast)
            s.activation(pm_sb[:, 1024:1024 + NT], M_sb[:], AF.Exp, bias=sm3[:])
            s.drain()
            s.nop().then_inc(sC.h, 1)
        sC.n += 1
        e16_done = sC.n

        @block.vector
        def _(v, e16_done=e16_done):
            v.wait_ge(sC.h, e16_done)
            v.tensor_tensor(pm_sb[:, 1024:1024 + NT], pm_sb[:, 1024:1024 + NT], L_sb[:], ALU.mult)
            v.drain()
            v.tensor_reduce(sm1[:], pm_sb[:, 1024:1024 + NT], AX.X, ALU.add)
            v.drain()
            v.nop().then_inc(sV.h, 1)
        sV.n += 1
        lz_done = sV.n

        @block.tensor
        def _(t, lz_done=lz_done):
            t.wait_ge(sV.h, lz_done)
            t.matmul(pb[1][0:1, 0:1], sm1[:], ones1[:], start=True, stop=True).then_inc(sT.h, 1)
        sT.n += 1
        zl_done = sT.n

        @block.vector
        def _(v, zl_done=zl_done):
            v.wait_ge(sT.h, zl_done)
            v.tensor_scalar(mzrow[:, 0:1], mzs[:, 0:1], -1.0, None, ALU.mult)
            v.tensor_copy(mzrow[:, 1:2], pb[1][0:1, 0:1])
            v.drain()
            v.nop().then_inc(sV.h, 1)
        sV.n += 1
        mzst = sV.n

        @block.sync
        def _(s, mzst=mzst):
            s.wait_ge(sV.h, mzst)
            s.dma_start(out=mz_in, in_=mzrow[:, 0:2]).then_inc(sS.h, 16)
        sS.n += 16
        mzd = sS.n

        @block.gpsimd
        def _(g, mzd=mzd):
            g.wait_ge(sS.h, mzd)
            g.collective_compute("AllGather", ALU.bypass,
                                 replica_groups=[list(range(NCORES))],
                                 ins=[mz_in], outs=[mz_all]).then_inc(sCC.h, 1)
        sCC.n += 1
        mzag = sCC.n

        @block.sync
        def _(s, mzag=mzag):
            s.wait_ge(sCC.h, mzag)
            s.dma_start(out=mzrow[:, :], in_=mz_all.rearrange("(p a) b -> p (a b)", p=1)).then_inc(sS.h, 16)
        sS.n += 16
        mzld = sS.n

        @block.vector
        def _(v, mzld=mzld):
            v.wait_ge(sS.h, mzld)
            v.tensor_reduce(mzred[:, :], mzrow[:, :].rearrange("p (a b) -> p b a", b=2),
                            AX.X, ALU.max)
            v.drain()
            v.tensor_scalar(mzs[:, :], mzred[:, 0:1], -1.0, None, ALU.mult)
            v.drain()
            v.nop().then_inc(sV.h, 1)
        sV.n += 1
        mstar = sV.n

        @block.scalar
        def _(s, mstar=mstar):
            s.wait_ge(sV.h, mstar)
            s.activation(mz8[:, :],
                         mzrow[:, :].rearrange("p (a b) -> p a b", b=2)[:, :, 0],
                         AF.Exp, bias=mzs[:, :])
            s.drain()
            s.nop().then_inc(sC.h, 1)
        sC.n += 1
        ez_done = sC.n

        @block.vector
        def _(v, ez_done=ez_done):
            v.wait_ge(sC.h, ez_done)
            v.tensor_tensor(mz8[:, :], mz8[:, :],
                            mzrow[:, :].rearrange("p (a b) -> p a b", b=2)[:, :, 1],
                            ALU.mult)
            v.drain()
            v.tensor_reduce(mzred[:, 1:2], mz8[:, :], AX.X, ALU.add)
            v.drain()
            v.reciprocal(mzred[:, 1:2], mzred[:, 1:2])
            v.drain()
            v.tensor_copy(mzrow[:, 2:3], mzs[:, 0:1])
            v.tensor_copy(mzrow[:, 3:4], mzred[:, 1:2])
            v.drain()
            v.nop().then_inc(sV.h, 1)
        sV.n += 1
        invz_done = sV.n

        @block.sync
        def _(s, invz_done=invz_done):
            s.wait_ge(sV.h, invz_done)
            s.dma_start(out=bscr, in_=mzrow[:, 2:4]).then_inc(sS.h, 16)
            s.wait_ge(sS.h, sS.n + 16)
            s.dma_start(out=bc2[:], in_=bscr.to_broadcast((128, 2))).then_inc(sS.h, 16)
        sS.n += 32
        bc_done = sS.n

        @block.scalar
        def _(s, bc_done=bc_done):
            s.wait_ge(sS.h, bc_done)
            s.activation(f_sb[:], M_sb[:], AF.Exp, bias=bc2[:, 0:1])
            s.drain()
            s.nop().then_inc(sC.h, 1)
        sC.n += 1
        f_exp = sC.n

        @block.vector
        def _(v, f_exp=f_exp):
            v.wait_ge(sC.h, f_exp)
            v.tensor_scalar(f_sb[:], f_sb[:], bc2[:, 1:2], None, ALU.mult)
            v.drain()
            v.nop().then_inc(sV.h, 1)
        sV.n += 1
        f_done = sV.n

        for j in range(NT):
            @block.sync
            def _(s, j=j, f_done=f_done):
                if j == 0:
                    s.wait_ge(sV.h, f_done)
                    s.wait_ge(sT.h, sT.n)
                s.dma_start(out=xs3(j), in_=xspill[j * 128:(j + 1) * 128, :]).then_inc(sS.h, 16)
            sS.n += 16
            xld = sS.n

            @block.vector
            def _(v, j=j, xld=xld):
                v.wait_ge(sS.h, xld)
                v.scalar_tensor_tensor(xs3(j), big[:, j * 1024:(j + 1) * 1024],
                                       f_sb[:, j:j + 1], xs3(j), ALU.mult, ALU.add)
                v.drain()
                v.nop().then_inc(sV.h, 1)
            sV.n += 1

        if nphases < 5:
            emit_tail()
            return nc

        emit_ffn("f2")
        emit_tail()
    return nc


def _limits_for_core(c):
    lim = np.full((LIMCOLS,), 1e4, np.float32)
    for vi, (kb, j) in enumerate(VISITS):
        lim[vi] = float(1024 * j + 128 * c - 512 * kb)
    return lim


def _shard_rows(c):
    tiles = [8 * j + c for j in range(NT)]
    return np.concatenate([np.arange(g * 128, (g + 1) * 128) for g in tiles])


def _pad_cw(cw):
    cwp = np.zeros((8192, 64), np.float32)
    cwp[:, 0:8] = np.asarray(cw, np.float32)
    t = np.arange(8192)
    cwp[:, 9] = t.astype(np.float32)
    return cwp


def _ptab():
    t = np.arange(8192)
    p = np.zeros((8192, 64), np.int32)
    p[:, 0] = (((t >> 7) & 7) << 10 | (t >> 10) << 7 | (t & 127)).astype(np.int32)
    return p


def make_in_maps(L0_attn_x, L0_attn_Wq, L0_attn_bq, L0_attn_Wk, L0_attn_bk,
                 L0_attn_Wv, L0_attn_bv, L0_ffn_combine_weights, L0_ffn_top_idx,
                 L0_ffn_W1, L0_ffn_b1, L0_ffn_W2, L0_ffn_b2):
    x = np.ascontiguousarray(np.asarray(L0_attn_x, np.float32).reshape(T, D))
    maps = []
    for c in range(NCORES):
        oh = np.zeros(64, np.float32)
        oh[c] = 1.0
        maps.append({
            "x_in": np.ascontiguousarray(x[_shard_rows(c)]),
            "wq": np.ascontiguousarray(np.asarray(L0_attn_Wq, np.float32)),
            "wk": np.ascontiguousarray(np.asarray(L0_attn_Wk, np.float32)),
            "wv": np.ascontiguousarray(np.asarray(L0_attn_Wv, np.float32)),
            "bq": np.ascontiguousarray(np.asarray(L0_attn_bq, np.float32)),
            "bk": np.ascontiguousarray(np.asarray(L0_attn_bk, np.float32)),
            "bv": np.ascontiguousarray(np.asarray(L0_attn_bv, np.float32)),
            "w1": np.ascontiguousarray(np.asarray(L0_ffn_W1, np.float32)[c]),
            "b1": np.ascontiguousarray(np.asarray(L0_ffn_b1, np.float32)[c]),
            "w2": np.ascontiguousarray(np.asarray(L0_ffn_W2, np.float32)[c]),
            "b2": np.ascontiguousarray(np.asarray(L0_ffn_b2, np.float32)[c]),
            "cw": _pad_cw(L0_ffn_combine_weights),
            "ptab": _ptab(),
            "oh128": np.tile(oh, 16),
            "idxc": np.ascontiguousarray(np.asarray(L0_ffn_top_idx, np.int32)[:, c]),
            "limits": _limits_for_core(c),
        })
    return maps


def kernel(**inputs):
    in_maps = make_in_maps(**inputs)
    nc = build_nc()
    mybir.codegen_inst_isa_subclasses(nc)
    res = run_bass_kernel_spmd(nc, in_maps, core_ids=list(range(NCORES)))
    out = np.empty((T, D), np.float32)
    for c in range(NCORES):
        out[_shard_rows(c)] = res.results[c]["x_out"]
    return out.reshape(1, T, D)
